# version 04
# baseline (speedup 1.0000x reference)
import contextlib
import numpy as np
import ml_dtypes
import concourse.bass as bass
import concourse.mybir as mybir
from concourse.bass_utils import run_bass_kernel_spmd

F32 = mybir.dt.float32
BF16 = mybir.dt.bfloat16
U8 = mybir.dt.uint8
AF = mybir.ActivationFunctionType
ALU = mybir.AluOpType
AX = mybir.AxisListType

B, T, C = 16, 2048, 1024
H, HS = 16, 64
E, FF = 8, 4096
NCORES = 8
BPC = B // NCORES
NTOK = BPC * T
NDC = C // 128
NFC = FF // 128
NJT = 3 * C // 128
NTILE = NTOK // 128
GT = 256
NG = NTOK // GT
TPG = GT // 128
EPS_A = 1e-5
EPS_M = 1e-6
BIG = 1.0e30


class Sched:
    ENG = ("tensor", "vector", "scalar", "sync", "gpsimd")

    def __init__(self):
        self.ops = {e: [] for e in self.ENG}
        self.cnt = {}
        self.dma_sems = set()

    def sem(self, name, dma=False):
        self.cnt.setdefault(name, 0)
        if dma:
            self.dma_sems.add(name)

    def now(self, name):
        return self.cnt[name]

    def add(self, eng, fn, waits=(), inc=None, amt=1):
        fixed = []
        for (s, v) in waits:
            if s in self.dma_sems:
                v = self.cnt[s]
            fixed.append((s, v))
        v = None
        if inc is not None:
            self.cnt[inc] += amt
            v = self.cnt[inc]
        self.ops[eng].append((fn, tuple(fixed), inc, amt))
        return v

    def dma(self, queue, out, in_, waits=(), inc=None):
        if inc is not None:
            self.sem(inc, dma=True)
        def fn(eng, out=out, in_=in_):
            return eng.dma_start(out=out, in_=in_)
        return self.add(queue, fn, waits=waits, inc=inc, amt=16)

    def barrier_counts(self):
        return dict(self.cnt)

    def emit(self, blk, sems, final=None):
        for name in self.ENG:
            items = list(self.ops[name])
            if not items and not (final and final[0] == name):
                continue

            def mk(items, name):
                def body(eng):
                    hwm = {}
                    for fn, waits, inc, amt in items:
                        for (s, v) in waits:
                            if v > 0 and hwm.get(s, -1) < v:
                                eng.wait_ge(sems[s], v)
                                hwm[s] = v
                        inst = fn(eng)
                        if inc is not None:
                            inst.then_inc(sems[inc], amt)
                    if final and final[0] == name:
                        eng.wait_ge(sems[final[1]], final[2])
                return body
            getattr(blk, name)(mk(items, name))


def build_nc(dev_moe_only=False, dbg_scratch=False):
    nc = bass.Bass("TRN2", target_bir_lowering=False, debug=False)

    dr = {}
    def din(name, shape, dtype=F32):
        dr[name] = nc.dram_tensor(name, list(shape), dtype, kind="ExternalInput").ap()
        return dr[name]

    din("x", (NTOK, C))
    din("wqkvT", (C, 3 * C))
    din("bqkv", (128, NJT))
    din("woutT", (C, C))
    din("rows8", (E, C))
    din("rw", (C, E))
    din("w1", (E, C, FF))
    din("b1c", (128, E * NFC))
    din("w2", (E, FF, C))
    din("b2", (E, C))
    din("gam", (E, C))
    din("bet", (E, C))
    din("allones", (128, 128))
    din("ones_col", (128, 1))
    din("oh8f", (E, E * 128))
    din("oh8b", (E, E * 128), BF16)
    din("id128b", (128, 128), BF16)
    din("id128f", (128, 128))
    din("id64sb", (128, 64), BF16)
    din("epsa", (128, 1))
    din("epsm", (128, 1))

    out_t = nc.dram_tensor("out", [NTOK, C], F32, kind="ExternalOutput").ap()

    skind = "ExternalInput" if dev_moe_only else (
        "ExternalOutput" if dbg_scratch else "Internal")
    x1_scr = nc.dram_tensor("x1_scr", [NTOK, C], F32, kind=skind).ap()
    x2_scr = nc.dram_tensor("x2_scr", [NTOK, C], F32, kind=skind).ap()
    x2fm_scr = nc.dram_tensor("x2fm_scr", [NG, NDC, 128, GT], BF16, kind=skind).ap()
    gates_scr = nc.dram_tensor("gates_scr", [NTILE, 128, E], F32, kind=skind).ap()
    gatesfm_scr = nc.dram_tensor("gatesfm_scr", [E, NTOK], F32, kind=skind).ap()
    y_scr = nc.dram_tensor("y_scr", [NTOK, C], F32, kind="Internal").ap()
    w1b_scr = nc.dram_tensor("w1b_scr", [E, C, FF], BF16, kind="Internal").ap()
    w2b_scr = nc.dram_tensor("w2b_scr", [E, FF, C], BF16, kind="Internal").ap()

    sd = Sched()
    for s in ("pe", "ve", "sc"):
        sd.sem(s)
    for s in ("lc", "wc", "lx0", "lx1", "lq0", "lq1", "sA", "lA0", "lA1",
              "sX1", "sX2", "sF", "sG", "sY", "ly", "lt0", "lt1",
              "lf0", "lf1", "wload", "lgf0", "lgf1", "sO", "wcv"):
        sd.sem(s, dma=True)

    with contextlib.ExitStack() as est:
        P = {}
        def palloc(name, shape, dtype=F32):
            P[name] = est.enter_context(nc.sbuf_tensor(name, list(shape), dtype))
            return P[name]
        palloc("ones_fs", (128, 128))
        palloc("ones_cs", (128, 1))
        palloc("oh8fs", (E, E * 128))
        palloc("oh8bs", (E, E * 128), BF16)
        palloc("id128bs", (128, 128), BF16)
        palloc("id128fs", (128, 128))
        palloc("id64sbs", (128, 64), BF16)
        palloc("epsas", (128, 1))
        palloc("epsms", (128, 1))
        palloc("gates_sb", (128, NTILE * E))
        palloc("b2s", (E, C), BF16)
        palloc("bets", (E, C), BF16)
        palloc("gamrows", (E, C))
        palloc("b1s", (128, E * NFC))

        for nm, src in (("ones_fs", "allones"), ("ones_cs", "ones_col"),
                        ("oh8fs", "oh8f"), ("id128bs", "id128b"),
                        ("id128fs", "id128f"), ("id64sbs", "id64sb"),
                        ("epsas", "epsa"), ("epsms", "epsm"),
                        ("gamrows", "gam")):
            sd.dma("sync", P[nm][:], dr[src], inc="lc")
        sd.dma("sync", P["b1s"][:], dr["b1c"], inc="lc")
        for nm, src in (("oh8bs", "oh8b"), ("b2s", "b2"), ("bets", "bet")):
            sd.dma("gpsimd", P[nm][:], dr[src], inc="wc")
        ld_consts = sd.now("lc")
        wl_consts = sd.now("wc")

        if dev_moe_only:
            dr["w1b_scr"] = w1b_scr
            dr["w2b_scr"] = w2b_scr
            for ci in range(16):
                e, hf = ci // 2, ci % 2
                sd.dma("gpsimd", w1b_scr[e, hf * 512:(hf + 1) * 512, :],
                       dr["w1"][e, hf * 512:(hf + 1) * 512, :], inc="wcv")
            for ci in range(16):
                e, hf = ci // 2, ci % 2
                sd.dma("gpsimd", w2b_scr[e, hf * 2048:(hf + 1) * 2048, :],
                       dr["w2"][e, hf * 2048:(hf + 1) * 2048, :], inc="wcv")
            for ti in range(NTILE):
                sd.dma("sync", P["gates_sb"][:, ti * E:(ti + 1) * E], gates_scr[ti],
                       inc="lc")
            barrier = sd.barrier_counts()
        else:
            av_scr = nc.dram_tensor("av_scr", [NDC, 128, T], BF16,
                                    kind=("ExternalOutput" if dbg_scratch
                                          else "Internal")).ap()
            dr["w1b_scr"] = w1b_scr
            dr["w2b_scr"] = w2b_scr
            barrier = build_attention(nc, sd, est, dr, P, x1_scr, x2_scr,
                                      x2fm_scr, gates_scr, gatesfm_scr, av_scr,
                                      ld_consts, wl_consts)

        build_moe(nc, sd, est, dr, P, x1_scr, x2_scr, x2fm_scr, y_scr, out_t,
                  barrier, ld_consts, wl_consts, gatesfm_scr)

        sems = {}
        for name in sd.cnt:
            sems[name] = est.enter_context(nc.semaphore(name))
        blk = est.enter_context(nc.Block("main"))
        sd.emit(blk, sems, final=("sync", "sO", sd.now("sO")))
    return nc


def build_moe(nc, sd, est, dr, P, x1_scr, x2_scr, x2fm_scr, y_scr, out_t,
              barrier, ld_consts, wl_consts, gatesfm_scr):
    moe_st = est.enter_context(contextlib.ExitStack())
    A = {}
    def alloc(name, shape, dtype=F32):
        A[name] = moe_st.enter_context(nc.sbuf_tensor(name, list(shape), dtype))
        return A[name]
    w1s = [alloc(f"w1s{i}", (128, FF), BF16) for i in range(NDC)]
    w2s = [alloc(f"w2s{i}", (128, C), BF16) for i in range(NFC)]
    hsb = [alloc(f"hsb{i}", (128, GT), BF16) for i in range(NFC)]
    x2f = [alloc(f"x2f{i}", (128, NDC * GT), BF16) for i in range(2)]
    x2t = [alloc(f"x2t{i}", (128, C)) for i in range(2)]
    ysb = [alloc(f"ysb{i}", (128, C)) for i in range(2)]
    res = [alloc(f"res{i}", (128, C)) for i in range(2)]
    gfm = [alloc(f"gfm{i}", (E, 128), BF16) for i in range(2)]
    gamr = alloc("gamr", (128, C))
    bnb = [alloc(f"bnb{i}", (128, 12)) for i in range(2)]
    st2 = [alloc(f"st2{i}", (128, 2)) for i in range(2)]
    rstd = [alloc(f"rstd{i}", (128, 1)) for i in range(2)]
    sgc = [alloc(f"sgc{i}", (128, 1)) for i in range(2)]
    bnc = [alloc(f"bnc{i}", (128, 1)) for i in range(2)]
    hp = [moe_st.enter_context(nc.psum_tensor(f"hp{i}", [128, 512], F32))
          for i in range(2)]
    op = [moe_st.enter_context(nc.psum_tensor(f"op{i}", [128, C], F32))
          for i in range(2)]
    mp = moe_st.enter_context(nc.psum_tensor("mp", [128, C], F32))

    bw = lambda s: [(s, barrier.get(s, 0))]
    dma_bar = [(s, 1) for s in sorted(sd.dma_sems)]
    bar_all = [(s, barrier.get(s, 0)) for s in ("pe", "ve", "sc")] + dma_bar

    NQ = 4
    w1_rd = [[0] * NDC for _ in range(E)]
    w2_rd = [[0] * NFC for _ in range(E)]
    relA = [0] * E
    relB = [[0] * NQ for _ in range(E)]
    pe_after_A = {}
    gelu_done = {}
    res_done = {}
    scn_done = {}
    out_done = {}
    gfm_used = {}
    y_store = {}
    gam_ready = [0] * E
    pending_fin = [None]

    def plan_finalize(e, g, peBs):
        for tt in range(TPG):
            ti = g * TPG + tt
            bi = (e * NG + g) * TPG + tt
            par = bi % 2
            peB = peBs[tt]
            xw = []
            if bi >= 2:
                xw.append(("ve", res_done[bi - 2]))
                if e == E - 1 and ti >= 2:
                    xw.append(("ve", out_done[ti - 2]))
            elif bi == 0:
                xw += bw("ve") + dma_bar
            x2ok = sd.dma("sync", x2t[par][:],
                          x2_scr[ti * 128:(ti + 1) * 128, :],
                          waits=xw + [("sX2", 1)], inc="lt%d" % par)
            if e > 0:
                yv = max(y_store.get((e, ti - 2), 0),
                         y_store.get((e - 1, ti), 0))
                yok = sd.dma("sync", ysb[par][:],
                             y_scr[ti * 128:(ti + 1) * 128, :],
                             waits=[("sY", yv), ("sO", 1)], inc="ly")
            else:
                yok = 0
            rw_ = [("pe", peB), ("lt%d" % par, x2ok)]
            if bi >= 2:
                rw_.append(("sc", scn_done[bi - 2]))
            ve_res = sd.add(
                "vector",
                (lambda par: lambda eng: eng.tensor_tensor(
                    res[par][:], op[par][:], x2t[par][:], ALU.add))(par),
                waits=rw_, inc="ve")
            res_done[bi] = ve_res
            sd.add("vector",
                   (lambda par: lambda eng: eng.bn_stats(
                       bnb[par][:, 0:6], res[par][:, 0:512]))(par),
                   waits=[("ve", ve_res)], inc="ve")
            ve_s2 = sd.add("vector",
                           (lambda par: lambda eng: eng.bn_stats(
                               bnb[par][:, 6:12], res[par][:, 512:1024]))(par),
                           inc="ve")
            ve_bn = sd.add("vector",
                           (lambda par: lambda eng: eng.bn_aggr(
                               st2[par][:], bnb[par][:, 0:12]))(par),
                           waits=[("ve", ve_s2)], inc="ve")
            sc_l = sd.add(
                "scalar",
                (lambda par: lambda eng: eng.activation(
                    out=rstd[par][:], in_=st2[par][:, 1:2],
                    func=AF.Ln, bias=P["epsms"][:]))(par),
                waits=[("ve", ve_bn)], inc="sc")
            sc_r = sd.add(
                "scalar",
                (lambda par: lambda eng: eng.activation(
                    out=rstd[par][:], in_=rstd[par][:],
                    func=AF.Exp, scale=-0.5))(par),
                waits=[("sc", sc_l)], inc="sc")
            ve_g1 = sd.add(
                "vector",
                (lambda e, ti, par: lambda eng: eng.tensor_tensor(
                    sgc[par][:], rstd[par][:],
                    P["gates_sb"][:, ti * E + e:ti * E + e + 1], ALU.mult))(e, ti, par),
                waits=[("sc", sc_r)], inc="ve")
            ve_sg = sd.add(
                "vector",
                (lambda par: lambda eng: eng.scalar_tensor_tensor(
                    bnc[par][:], st2[par][:, 0:1], -1.0, sgc[par][:],
                    ALU.mult, ALU.mult))(par),
                waits=[("ve", ve_g1)], inc="ve")
            sc_n = sd.add(
                "scalar",
                (lambda par: lambda eng: eng.activation(
                    out=res[par][:], in_=res[par][:], func=AF.Identity,
                    scale=sgc[par][:], bias=bnc[par][:]))(par),
                waits=[("ve", ve_sg)], inc="sc")
            scn_done[bi] = sc_n

            if e == 0:
                ve_y = sd.add(
                    "vector",
                    (lambda par: lambda eng: eng.tensor_tensor(
                        ysb[par][:], res[par][:], gamr[:], ALU.mult))(par),
                    waits=[("sc", sc_n), ("ve", gam_ready[e]), ("sY", 1)],
                    inc="ve")
            else:
                ve_t = sd.add(
                    "vector",
                    (lambda par: lambda eng: eng.tensor_tensor(
                        res[par][:], res[par][:], gamr[:], ALU.mult))(par),
                    waits=[("sc", sc_n), ("ve", gam_ready[e])], inc="ve")
                ve_y = sd.add(
                    "vector",
                    (lambda par: lambda eng: eng.tensor_tensor(
                        ysb[par][:], ysb[par][:], res[par][:], ALU.add))(par),
                    waits=[("ve", ve_t), ("ly", yok)], inc="ve")

            if e < E - 1:
                y_store[(e, ti)] = sd.dma(
                    "sync", y_scr[ti * 128:(ti + 1) * 128, :], ysb[par][:],
                    waits=[("ve", ve_y)], inc="sY")
            else:
                gfmok = sd.dma("gpsimd", gfm[par][:],
                               gatesfm_scr[:, ti * 128:(ti + 1) * 128],
                               waits=[("pe", gfm_used.get(ti - 2, 0)),
                                      ("sG", 1)],
                               inc="lgf%d" % par)
                def mkBeta(ti, par):
                    def f(eng):
                        for dh in range(2):
                            mi = eng.matmul(
                                mp[:, dh * 512:(dh + 1) * 512],
                                gfm[par][:],
                                P["bets"][:, dh * 512:(dh + 1) * 512],
                                start=True, stop=True)
                        return mi
                    return f
                pe_bg = sd.add("tensor", mkBeta(ti, par),
                               waits=[("ve", ve_y), ("wc", wl_consts),
                                      ("lgf%d" % par, gfmok)], inc="pe")
                gfm_used[ti] = pe_bg
                ve_yt = sd.add(
                    "vector",
                    (lambda par: lambda eng: eng.tensor_tensor(
                        ysb[par][:], ysb[par][:], mp[:], ALU.add))(par),
                    waits=[("pe", pe_bg)], inc="ve")
                sd.add("vector",
                       (lambda par: lambda eng: eng.bn_stats(
                           bnb[par][:, 0:6], ysb[par][:, 0:512]))(par),
                       waits=[("ve", ve_yt)], inc="ve")
                ve_q2 = sd.add("vector",
                               (lambda par: lambda eng: eng.bn_stats(
                                   bnb[par][:, 6:12], ysb[par][:, 512:1024]))(par),
                               inc="ve")
                ve_bn2 = sd.add("vector",
                                (lambda par: lambda eng: eng.bn_aggr(
                                    st2[par][:], bnb[par][:, 0:12]))(par),
                                waits=[("ve", ve_q2)], inc="ve")
                sc_l2 = sd.add(
                    "scalar",
                    (lambda par: lambda eng: eng.activation(
                        out=rstd[par][:], in_=st2[par][:, 1:2],
                        func=AF.Ln, bias=P["epsas"][:]))(par),
                    waits=[("ve", ve_bn2)], inc="sc")
                sc_r2 = sd.add(
                    "scalar",
                    (lambda par: lambda eng: eng.activation(
                        out=rstd[par][:], in_=rstd[par][:],
                        func=AF.Exp, scale=-0.5))(par),
                    waits=[("sc", sc_l2)], inc="sc")
                ve_b2 = sd.add(
                    "vector",
                    (lambda par: lambda eng: eng.scalar_tensor_tensor(
                        bnc[par][:], st2[par][:, 0:1], -1.0, rstd[par][:],
                        ALU.mult, ALU.mult))(par),
                    waits=[("sc", sc_r2)], inc="ve")
                sc_n2 = sd.add(
                    "scalar",
                    (lambda par: lambda eng: eng.activation(
                        out=res[par][:], in_=ysb[par][:], func=AF.Identity,
                        scale=rstd[par][:], bias=bnc[par][:]))(par),
                    waits=[("ve", ve_b2)], inc="sc")
                scn_done[bi] = sc_n2
                x1ok = sd.dma("sync", x2t[par][:],
                              x1_scr[ti * 128:(ti + 1) * 128, :],
                              waits=[("ve", res_done[bi]), ("sX1", 1)],
                              inc="lt%d" % par)
                ve_out = sd.add(
                    "vector",
                    (lambda par: lambda eng: eng.tensor_tensor(
                        ysb[par][:], res[par][:], x2t[par][:], ALU.add))(par),
                    waits=[("sc", sc_n2), ("lt%d" % par, x1ok)], inc="ve")
                out_done[ti] = ve_out
                y_store[(e, ti)] = sd.dma(
                    "sync", out_t[ti * 128:(ti + 1) * 128, :], ysb[par][:],
                    waits=[("ve", ve_out)], inc="sO")

    for e in range(E):
        for dc in range(NDC):
            w = (bar_all + [("wcv", 1)]) if (e == 0 and dc == 0) else \
                ([("pe", relA[e - 1]), ("wcv", 1)] if (e > 0 and dc == 0) else ())
            w1_rd[e][dc] = sd.dma("gpsimd", w1s[dc][:],
                                  dr["w1b_scr"][e, dc * 128:(dc + 1) * 128, :],
                                  waits=w, inc="wload")
        for fc in range(NFC):
            q = fc // (NFC // NQ)
            w = [("pe", relB[e - 1][q]), ("wcv", 1)] \
                if (e > 0 and fc % (NFC // NQ) == 0) else ()
            w2_rd[e][fc] = sd.dma("gpsimd", w2s[fc][:],
                                  dr["w2b_scr"][e, fc * 128:(fc + 1) * 128, :],
                                  waits=w, inc="wload")

        for g in range(NG):
            gi = e * NG + g
            xw = list(dma_bar) if gi == 0 else []
            if gi >= 2:
                xw.append(("pe", pe_after_A[gi - 2]))
            x2f_ok = sd.dma("sync",
                            x2f[gi % 2][:].rearrange("p (dc t) -> p dc t", dc=NDC),
                            x2fm_scr[g].rearrange("dc p t -> p dc t"),
                            waits=xw, inc="lf%d" % (gi % 2))

            for fc in range(NFC):
                it = gi * NFC + fc
                waits = [("wload", w1_rd[e][NDC - 1]),
                         ("lf%d" % (gi % 2), x2f_ok)]
                if it >= 2:
                    waits.append(("sc", gelu_done[it - 2]))
                def mkA(gi, fc):
                    def f(eng):
                        for dc in range(NDC):
                            mi = eng.matmul(
                                hp[fc % 2][:, 0:GT],
                                w1s[dc][:, fc * 128:(fc + 1) * 128],
                                x2f[gi % 2][:, dc * GT:(dc + 1) * GT],
                                start=(dc == 0), stop=(dc == NDC - 1))
                        return mi
                    return f
                peA = sd.add("tensor", mkA(gi, fc), waits=waits, inc="pe")
                gelu_done[it] = sd.add(
                    "scalar",
                    (lambda e, fc: lambda eng: eng.activation(
                        out=hsb[fc][:], in_=hp[fc % 2][:, 0:GT], func=AF.Gelu,
                        bias=P["b1s"][:, e * NFC + fc:e * NFC + fc + 1]))(e, fc),
                    waits=[("pe", peA)], inc="sc")
            pe_after_A[gi] = sd.now("pe")
            if g == NG - 1:
                relA[e] = pe_after_A[gi]

            if pending_fin[0] is not None:
                pending_fin[0]()
                pending_fin[0] = None

            if g == 0:
                gw = [("wc", wl_consts), ("lc", ld_consts), ("ve", sd.now("ve"))]
                if e == 0:
                    gw += bar_all
                def mk_g(e):
                    def f(eng):
                        for dh in range(2):
                            mi = eng.matmul(mp[:, dh * 512:(dh + 1) * 512],
                                            P["oh8fs"][:, e * 128:(e + 1) * 128],
                                            P["gamrows"][:, dh * 512:(dh + 1) * 512],
                                            start=True, stop=True)
                        return mi
                    return f
                pe_g = sd.add("tensor", mk_g(e), waits=gw, inc="pe")
                gam_ready[e] = sd.add(
                    "vector", lambda eng: eng.tensor_copy(out=gamr[:], in_=mp[:]),
                    waits=[("pe", pe_g)], inc="ve")

            peBs = []
            last_grp = (g == NG - 1)
            for tt in range(TPG):
                ti = g * TPG + tt
                bi = gi * TPG + tt
                par = bi % 2
                waits = [("sc", gelu_done[gi * NFC + NFC - 1]),
                         ("wload", w2_rd[e][NFC - 1])]
                if bi >= 2:
                    waits.append(("ve", res_done[bi - 2]))
                mark_q = last_grp and tt == TPG - 1
                qsz = NFC // NQ
                for q in range(NQ):
                    def mkB(e, par, tt, q, qsz):
                        def f(eng):
                            for fc in range(q * qsz, (q + 1) * qsz):
                                for dh in range(2):
                                    mi = eng.matmul(
                                        op[par][:, dh * 512:(dh + 1) * 512],
                                        hsb[fc][:, tt * 128:(tt + 1) * 128],
                                        w2s[fc][:, dh * 512:(dh + 1) * 512],
                                        start=(fc == 0), stop=False)
                            if q == NQ - 1:
                                for dh in range(2):
                                    mi = eng.matmul(
                                        op[par][:, dh * 512:(dh + 1) * 512],
                                        P["oh8bs"][:, e * 128:(e + 1) * 128],
                                        P["b2s"][:, dh * 512:(dh + 1) * 512],
                                        start=False, stop=True)
                            return mi
                        return f
                    peB = sd.add("tensor", mkB(e, par, tt, q, qsz),
                                 waits=waits if q == 0 else (), inc="pe")
                    if mark_q:
                        relB[e][q] = peB
                peBs.append(peB)
            pending_fin[0] = (lambda e, g, peBs: lambda: plan_finalize(e, g, peBs))(e, g, peBs)

    pending_fin[0]()
    pending_fin[0] = None


DBG = [False]


def build_attention(nc, sd, est, dr, P, x1_scr, x2_scr, x2fm_scr,
                    gates_out_scr, gatesfm_scr, av_scr, ld_consts, wl_consts):
    ast = contextlib.ExitStack()
    A = {}
    def alloc(name, shape, dtype=F32):
        A[name] = ast.enter_context(nc.sbuf_tensor(name, list(shape), dtype))
        return A[name]

    kv = alloc("kv", (128, 16 * T), BF16)
    qring = [alloc(f"qring{i}", (128, T), BF16) for i in range(2)]
    lnfm = alloc("lnfm", (128, NDC * T), BF16)
    woT = alloc("woT", (128, NDC * C), BF16)
    wqb = [alloc(f"wqb{i}", (128, NDC * 128), BF16) for i in range(2)]
    vh = alloc("vh", (128, 16 * 65), BF16)
    pT = [alloc(f"pT{i}", (128, 1024), BF16) for i in range(2)]
    q2b = alloc("q2b", (128, T), BF16)
    avb = [alloc(f"avb{i}", (128, NDC * 128), BF16) for i in range(2)]
    xres = [alloc(f"xres{i}", (128, C)) for i in range(2)]
    x1s = [alloc(f"x1s{i}", (128, C)) for i in range(2)]
    x2s = [alloc(f"x2s{i}", (128, C)) for i in range(2)]
    lnb = [alloc(f"lnb{i}", (128, C), BF16) for i in range(2)]
    avst = lnb
    sumbc = alloc("sumbc", (128, 1024))
    x2fT0 = alloc("x2fT0", (128, C))
    x2fT = [x2fT0, x2fT0]
    x2fg = [alloc(f"x2fg{i}", (128, NDC * GT), BF16) for i in range(2)]
    rws = alloc("rws", (128, NDC * E))
    rows8b = alloc("rows8b", (E, C), BF16)
    bqkvs = alloc("bqkvs", (128, NJT))
    gfmst = [alloc(f"gfmst{i}", (E, 128)) for i in range(2)]
    bnb = [alloc(f"abnb{i}", (128, 12)) for i in range(2)]
    st2 = [alloc(f"ast2{i}", (128, 2)) for i in range(2)]
    rstd = [alloc(f"arstd{i}", (128, 1)) for i in range(2)]
    bnc = [alloc(f"abnc{i}", (128, 1)) for i in range(2)]
    rm = alloc("rm", (128, 2))
    csb = alloc("csb", (1, 1))
    ebias = [alloc(f"ebias{i}", (128, 1)) for i in range(2)]
    glog = alloc("glog", (128, E))
    gm1 = alloc("gm1", (128, 1))
    gm2 = alloc("gm2", (128, 1))
    gmask1 = alloc("gmask1", (128, E))
    gmask2 = alloc("gmask2", (128, E))
    ge2 = alloc("ge2", (128, 1))
    gden = alloc("gden", (128, 1))
    g1c = alloc("g1c", (128, 1))

    ps0 = ast.enter_context(nc.psum_tensor("aps0", [128, 1024], F32))
    ps1 = ast.enter_context(nc.psum_tensor("aps1", [128, 1024], F32))
    psav = ast.enter_context(nc.psum_tensor("apsav", [128, 1024], F32))
    pv = ast.enter_context(nc.psum_tensor("apv", [128, 1024], BF16))

    ONES = P["ones_fs"]
    dbg = nc.dram_tensor("dbg_kv", [2, 128, 16 * T], BF16,
                         kind="ExternalOutput").ap() if DBG[0] else None

    sd.dma("gpsimd", woT[:].rearrange("p (dc c) -> p dc c", dc=NDC),
           dr["woutT"].rearrange("(dc p) c -> p dc c", p=128), inc="wc")
    sd.dma("gpsimd", rows8b[:], dr["rows8"], inc="wc")
    sd.dma("sync", bqkvs[:], dr["bqkv"], inc="lc")
    sd.dma("sync", rws[:].rearrange("p (dc e) -> p dc e", dc=NDC),
           dr["rw"].rearrange("(dc p) e -> p dc e", p=128), inc="lc")
    vh_set = sd.add("vector", lambda eng: eng.memset(vh[:], 1.0), inc="ve")

    lnfm3 = lnfm[:].rearrange("p (dc t) -> p dc t", dc=NDC)

    wcv_chunks = []
    for e in range(E):
        for hf in range(2):
            wcv_chunks.append((1, e, hf))
            wcv_chunks.append((2, e, hf))

    lnfm_done = {}
    head_exp_end = {}
    vm_done = {}
    batch_prev_ve = 0
    evacA = {}
    qkv_evac = {}
    wq_used = {}
    pv_free = {}
    head_pe_end = {}
    head_sc_end = {}
    avdma = {}
    va_done = {}
    x2f_stored = {}
    gfm_used2 = {}
    proj_end_pe = 0
    pvi = 0
    peT_done = {}

    for b in range(BPC):
        xoff = b * T

        for i in range(T // 128):
            par = i % 2
            xw = [("sc", evacA.get((b, i - 2), 0))]
            xok = sd.dma("sync", xres[par][:],
                         dr["x"][xoff + i * 128: xoff + (i + 1) * 128, :],
                         waits=xw, inc="lx%d" % par)
            v1 = sd.add("vector",
                        (lambda par: lambda eng: eng.bn_stats(
                            bnb[par][:, 0:6], xres[par][:, 0:512]))(par),
                        waits=[("lx%d" % par, xok),
                               ("sc", evacA.get((b, i - 2), 0))],
                        inc="ve")
            v2 = sd.add("vector",
                        (lambda par: lambda eng: eng.bn_stats(
                            bnb[par][:, 6:12], xres[par][:, 512:1024]))(par),
                        inc="ve")
            v3 = sd.add("vector",
                        (lambda par: lambda eng: eng.bn_aggr(
                            st2[par][:], bnb[par][:, 0:12]))(par),
                        waits=[("ve", v2)], inc="ve")
            s1 = sd.add("scalar",
                        (lambda par: lambda eng: eng.activation(
                            out=rstd[par][:], in_=st2[par][:, 1:2], func=AF.Ln,
                            bias=P["epsas"][:]))(par),
                        waits=[("ve", v3)], inc="sc")
            s2 = sd.add("scalar",
                        (lambda par: lambda eng: eng.activation(
                            out=rstd[par][:], in_=rstd[par][:], func=AF.Exp,
                            scale=-0.5))(par),
                        waits=[("sc", s1)], inc="sc")
            v4 = sd.add("vector",
                        (lambda par: lambda eng: eng.scalar_tensor_tensor(
                            bnc[par][:], st2[par][:, 0:1], -1.0, rstd[par][:],
                            ALU.mult, ALU.mult))(par),
                        waits=[("sc", s2)], inc="ve")
            ew = [("ve", v4), ("pe", peT_done.get((b, i - 2), 0)), ("sA", 1)]
            s3 = sd.add("scalar",
                        (lambda par: lambda eng: eng.activation(
                            out=lnb[par][:], in_=xres[par][:], func=AF.Identity,
                            scale=rstd[par][:], bias=bnc[par][:]))(par),
                        waits=ew, inc="sc")
            evacA[(b, i)] = s3
            tw = [("sc", s3), ("ve", lnfm_done.get((b, i - 1), vh_set))]
            if b > 0 and i == 0:
                tw.append(("pe", proj_end_pe))
            def mkT(par):
                def f(eng):
                    for dc in range(NDC):
                        mi = eng.transpose(pv[:, dc * 128:(dc + 1) * 128],
                                           lnb[par][:, dc * 128:(dc + 1) * 128],
                                           P["id128bs"][:])
                    return mi
                return f
            peT = sd.add("tensor", mkT(par), waits=tw, inc="pe")
            peT_done[(b, i)] = peT
            lnfm_done[(b, i)] = sd.add(
                "vector",
                (lambda i: lambda eng: eng.tensor_copy(
                    out=lnfm3[:, :, i * 128:(i + 1) * 128],
                    in_=pv[:].rearrange("p (dc t) -> p dc t", dc=NDC)))(i),
                waits=[("pe", peT)], inc="ve")
        lnfm_all = lnfm_done[(b, T // 128 - 1)]

        def qkv_tile(jt, seq, dest_ap, extra_w=()):
            lw = list(extra_w)
            lw.append(("pe", wq_used.get((b, seq - 2), 0)))
            wok = sd.dma("gpsimd",
                         wqb[seq % 2][:].rearrange("p (dc j) -> p dc j", dc=NDC),
                         dr["wqkvT"][:, jt * 128:(jt + 1) * 128].rearrange(
                             "(dc p) j -> p dc j", p=128),
                         waits=lw, inc="lq%d" % (seq % 2))
            if wcv_chunks:
                which, e, hf = wcv_chunks.pop(0)
                if which == 1:
                    sd.dma("gpsimd", dr["w1b_scr"][e, hf * 512:(hf + 1) * 512, :],
                           dr["w1"][e, hf * 512:(hf + 1) * 512, :], inc="wcv")
                else:
                    sd.dma("gpsimd", dr["w2b_scr"][e, hf * 2048:(hf + 1) * 2048, :],
                           dr["w2"][e, hf * 2048:(hf + 1) * 2048, :], inc="wcv")
            last = 0
            for tc in range(4):
                half = tc % 2
                def mkQ(seq, tc, half):
                    def f(eng):
                        for dc in range(NDC):
                            mi = eng.matmul(
                                ps0[:, half * 512:(half + 1) * 512],
                                wqb[seq % 2][:, dc * 128:(dc + 1) * 128],
                                lnfm3[:, dc, tc * 512:(tc + 1) * 512],
                                start=(dc == 0), stop=(dc == NDC - 1))
                        return mi
                    return f
                qw = [("lq%d" % (seq % 2), wok), ("ve", lnfm_all)] + list(extra_w)
                if tc >= 2:
                    qw.append(("sc", last))
                pe_q = sd.add("tensor", mkQ(seq, tc, half), waits=qw, inc="pe")
                wq_used[(b, seq)] = pe_q
                last = sd.add(
                    "scalar",
                    (lambda jt, tc, half, dest_ap: lambda eng: eng.add(
                        out=dest_ap[:, tc * 512:(tc + 1) * 512],
                        in_=ps0[:, half * 512:(half + 1) * 512],
                        add=bqkvs[:, jt:jt + 1]))(jt, tc, half, dest_ap),
                    waits=[("pe", pe_q)], inc="sc")
            qkv_evac[jt] = last
            return last

        kv3 = kv[:].rearrange("p (n t) -> p n t", n=16)
        for n in range(16):
            jt = 8 + n
            ew = []
            if b > 0 and n == 0:
                ew = [("pe", head_pe_end.get((b - 1, H - 1), 0)),
                      ("sc", head_exp_end.get((b - 1, H - 1), 0)),
                      ("ve", batch_prev_ve)]
            qkv_tile(jt, n, kv3[:, n, :], extra_w=ew)

        if dbg is not None:
            sd.dma("sync", dbg[b], kv[:],
                   waits=[("sc", qkv_evac[23])], inc="sG")

        for p in range(H // 2):
            ew = [("pe", head_pe_end.get((b, 2 * (p - 1) + 1), 0)),
                  ("sc", head_exp_end.get((b, 2 * (p - 1) + 1), 0)),
                  ("ve", va_done.get((b, 2 * p - 1), 0))]
            q_evac = qkv_tile(p, 16 + p, qring[p % 2][:], extra_w=ew)

            for h in (2 * p, 2 * p + 1):
                base = (h % 2) * 64
                hp2 = h % 2
                kslice = kv3[:, h // 2, :]
                vslice = kv3[:, 8 + h // 2, :]
                qslice = qring[p % 2][:]

                vw = [("sc", qkv_evac[16 + h // 2]),
                      ("ve", pv_free.get(pvi - 1, vh_set))]
                def mkV(base, vsl):
                    def f(eng):
                        for st in range(16):
                            mi = eng.transpose(
                                pv[:, st * 64:(st + 1) * 64],
                                vsl[base:base + 64, st * 128:(st + 1) * 128],
                                P["id64sbs"][base:base + 64, 0:64])
                        return mi
                    return f
                peV = sd.add("tensor", mkV(base, vslice), waits=vw, inc="pe")
                ve_vh = sd.add(
                    "vector",
                    lambda eng: eng.tensor_copy(
                        out=vh[:].rearrange("p (n c) -> p n c", n=16)[:, :, 0:64],
                        in_=pv[:].rearrange("p (n c) -> p n c", n=16)),
                    waits=[("pe", peV), ("pe", head_pe_end.get((b, h - 1), 0))],
                    inc="ve")
                pv_free[pvi] = ve_vh
                pvi += 1

                pw2 = vm_done.get((b, h - 1), (0, 0))
                sq1 = sd.add(
                    "vector",
                    (lambda base, qsl: lambda eng: eng.tensor_tensor(
                        q2b[0:64, :], qsl[base:base + 64, :],
                        qsl[base:base + 64, :], ALU.mult))(base, qslice),
                    waits=[("sc", q_evac), ("ve", pw2[0])], inc="ve")
                sq2 = sd.add(
                    "vector",
                    (lambda base, ksl: lambda eng: eng.tensor_tensor(
                        q2b[64:128, :], ksl[base:base + 64, :],
                        ksl[base:base + 64, :], ALU.mult))(base, kslice),
                    waits=[("sc", qkv_evac[8 + h // 2]), ("ve", pw2[1])], inc="ve")
                head_sc_end[(b, h)] = sd.now("sc")
                vm1 = sd.add(
                    "vector",
                    (lambda: lambda eng: eng.reduce_max(
                        rm[0:64, 0:1], q2b[0:64, :], axis=AX.X))(),
                    waits=[("ve", sq1), ("pe", head_pe_end.get((b, h - 1), 0))],
                    inc="ve")
                vm2 = sd.add(
                    "vector",
                    (lambda: lambda eng: eng.reduce_max(
                        rm[64:128, 1:2], q2b[64:128, :], axis=AX.X))(),
                    waits=[("ve", sq2)], inc="ve")
                vm_done[(b, h)] = (vm1, vm2)
                def mkC(base):
                    def f(eng):
                        eng.matmul(psav[0:1, 0:1], P["ones_cs"][0:64, :],
                                   rm[0:64, 0:1], start=True, stop=False)
                        return eng.matmul(psav[0:1, 0:1],
                                          P["ones_cs"][64:128, :],
                                          rm[64:128, 1:2],
                                          start=False, stop=True)
                    return f
                peC = sd.add("tensor", mkC(base),
                             waits=[("ve", vm2), ("ve", vm1),
                                    ("ve", va_done.get((b, h - 1), 0))], inc="pe")
                vcs = sd.add(
                    "vector",
                    lambda eng: eng.tensor_scalar(csb[:], psav[0:1, 0:1],
                                                  -0.0625, None, ALU.mult),
                    waits=[("pe", peC)], inc="ve")
                peCB = sd.add(
                    "tensor",
                    lambda eng: eng.matmul(psav[:, 8:9], ONES[0:1, 0:128],
                                           csb[:], start=True, stop=True),
                    waits=[("ve", vcs)], inc="pe")
                veEB = sd.add(
                    "vector",
                    (lambda hp2: lambda eng: eng.tensor_copy(
                        out=ebias[hp2][:], in_=psav[:, 8:9]))(hp2),
                    waits=[("pe", peCB)], inc="ve")

                for th in range(2):
                    exp_done = {}
                    av_done = {}
                    for st in range(16):
                        spar = st % 2
                        sw = [("ve", veEB)]
                        if st >= 2:
                            sw.append(("sc", exp_done[st - 2]))
                        if st == 0:
                            sw.append(("ve", sd.now("ve")))
                        def mkS(base, ksl, qsl, th, st, spar):
                            def f(eng):
                                for dh in range(2):
                                    mi = eng.matmul(
                                        (ps0 if spar == 0 else ps1)[:, dh * 512:(dh + 1) * 512],
                                        ksl[base:base + 64, st * 128:(st + 1) * 128],
                                        qsl[base:base + 64,
                                            th * 1024 + dh * 512: th * 1024 + (dh + 1) * 512],
                                        start=True, stop=True)
                                return mi
                            return f
                        peS = sd.add("tensor", mkS(base, kslice, qslice, th, st, spar),
                                     waits=sw, inc="pe")
                        exp_done[st] = sd.add(
                            "scalar",
                            (lambda spar, hp2: lambda eng: eng.activation(
                                out=pT[spar][:], in_=(ps0 if spar == 0 else ps1)[:],
                                func=AF.Exp, scale=0.125,
                                bias=ebias[hp2][:]))(spar, hp2),
                            waits=[("pe", peS)], inc="sc")
                        aw = [("sc", exp_done[st]), ("ve", ve_vh)]
                        def mkAV(st, spar):
                            def f(eng):
                                for dh in range(2):
                                    mi = eng.matmul(
                                        psav[0:65, dh * 512:(dh + 1) * 512],
                                        vh[:, st * 65:(st + 1) * 65],
                                        pT[spar][:, dh * 512:(dh + 1) * 512],
                                        start=(st == 0), stop=(st == 15))
                                return mi
                            return f
                        av_done[st] = sd.add("tensor", mkAV(st, spar),
                                             waits=aw, inc="pe")
                    vr0 = sd.add(
                        "scalar",
                        lambda eng: eng.activation(
                            out=sumbc[64:65, 0:1024], in_=psav[64:65, 0:1024],
                            func=AF.Ln),
                        waits=[("pe", av_done[15])], inc="sc")
                    vr = sd.add(
                        "scalar",
                        lambda eng: eng.activation(
                            out=sumbc[64:65, 0:1024], in_=sumbc[64:65, 0:1024],
                            func=AF.Exp, scale=-1.0),
                        waits=[("sc", vr0)], inc="sc")
                    def mkBC(eng):
                        for dh in range(2):
                            mi = eng.matmul(ps0[0:64, dh * 512:(dh + 1) * 512],
                                            ONES[64:65, 0:64],
                                            sumbc[64:65, dh * 512:(dh + 1) * 512],
                                            start=True, stop=True)
                        return mi
                    peBC = sd.add("tensor", mkBC, waits=[("sc", vr)], inc="pe")
                    vbc = sd.add(
                        "vector",
                        lambda eng: eng.tensor_copy(out=sumbc[0:64, :],
                                                    in_=ps0[0:64, :]),
                        waits=[("pe", peBC)], inc="ve")
                    va = sd.add(
                        "vector",
                        (lambda base, th, hp2: lambda eng: eng.tensor_tensor(
                            avst[hp2][base:base + 64, :], psav[0:64, :],
                            sumbc[0:64, :], ALU.mult))(base, th, hp2),
                        waits=[("ve", vbc), ("sA", 1)], inc="ve")
                    avdma[(b, h, th)] = sd.dma(
                        "sync",
                        av_scr[h // 2, base:base + 64,
                               th * 1024:(th + 1) * 1024],
                        avst[hp2][base:base + 64, :],
                        waits=[("ve", va)], inc="sA")
                    va_done[(b, h)] = va
                    head_pe_end[(b, h)] = peBC
                head_exp_end[(b, h)] = sd.now("sc")

        for i in range(T // 128):
            ti = b * (T // 128) + i
            par = i % 2
            aw = [("sA", 1), ("pe", lnfm_done.get(("peP", b, i - 2), 0))]
            avok = sd.dma("sync",
                          avb[par][:].rearrange("p (dc t) -> p dc t", dc=NDC),
                          av_scr[:, :, i * 128:(i + 1) * 128].rearrange(
                              "dc p t -> p dc t"),
                          waits=aw, inc="lA%d" % par)
            if i < 2:
                xw2 = [("sc", evacA.get((b, 14 + i), 0))]
            else:
                xw2 = [("ve", lnfm_done.get(("x1", b, i - 2), 0))]
            xok = sd.dma("sync", xres[par][:],
                         dr["x"][xoff + i * 128: xoff + (i + 1) * 128, :],
                         waits=xw2, inc="lx%d" % par)
            woT3 = woT[:].rearrange("p (dc c) -> p dc c", dc=NDC)
            def mkP(par):
                def f(eng):
                    for dh in range(2):
                        for dc in range(NDC):
                            eng.matmul(ps1[:, dh * 512:(dh + 1) * 512],
                                       avb[par][:, dc * 128:(dc + 1) * 128],
                                       woT3[:, dc, dh * 512:(dh + 1) * 512],
                                       start=(dc == 0), stop=False)
                        mi = eng.matmul(ps1[:, dh * 512:(dh + 1) * 512],
                                        P["oh8bs"][:, 0:128],
                                        rows8b[:, dh * 512:(dh + 1) * 512],
                                        start=False, stop=True)
                    return mi
                return f
            pw = [("lA%d" % par, avok), ("wc", wl_consts)]
            if i == 0:
                pw.append(("pe", head_pe_end[(b, H - 1)]))
                pw.append(("sc", head_sc_end[(b, H - 1)]))
            if i >= 1:
                pw.append(("sc", lnfm_done.get(("sc_d", b, i - 1), 0)))
            peP = sd.add("tensor", mkP(par), waits=pw, inc="pe")
            proj_end_pe = peP
            lnfm_done[("peP", b, i)] = peP
            v1 = sd.add("vector",
                        (lambda par: lambda eng: eng.bn_stats(
                            bnb[par][:, 0:6], ps1[:, 0:512]))(par),
                        waits=[("pe", peP)], inc="ve")
            v2 = sd.add("vector",
                        (lambda par: lambda eng: eng.bn_stats(
                            bnb[par][:, 6:12], ps1[:, 512:1024]))(par),
                        inc="ve")
            v3 = sd.add("vector",
                        (lambda par: lambda eng: eng.bn_aggr(
                            st2[par][:], bnb[par][:, 0:12]))(par),
                        waits=[("ve", v2)], inc="ve")
            s1 = sd.add("scalar",
                        (lambda par: lambda eng: eng.activation(
                            out=rstd[par][:], in_=st2[par][:, 1:2], func=AF.Ln,
                            bias=P["epsas"][:]))(par),
                        waits=[("ve", v3)], inc="sc")
            s2 = sd.add("scalar",
                        (lambda par: lambda eng: eng.activation(
                            out=rstd[par][:], in_=rstd[par][:], func=AF.Exp,
                            scale=-0.5))(par),
                        waits=[("sc", s1)], inc="sc")
            v4 = sd.add("vector",
                        (lambda par: lambda eng: eng.scalar_tensor_tensor(
                            bnc[par][:], st2[par][:, 0:1], -1.0, rstd[par][:],
                            ALU.mult, ALU.mult))(par),
                        waits=[("sc", s2)], inc="ve")
            v5 = sd.add("vector",
                        (lambda par: lambda eng: eng.tensor_scalar(
                            xres[par][:], xres[par][:], bnc[par][:], None,
                            ALU.add))(par),
                        waits=[("ve", v4), ("lx%d" % par, xok)], inc="ve")
            v6 = sd.add("vector",
                        (lambda par: lambda eng: eng.scalar_tensor_tensor(
                            x1s[par][:], ps1[:], rstd[par][:], xres[par][:],
                            ALU.mult, ALU.add))(par),
                        waits=[("ve", v5), ("sX1", 1)], inc="ve")
            lnfm_done[("x1", b, i)] = v6
            sd.dma("sync", x1_scr[ti * 128:(ti + 1) * 128, :], x1s[par][:],
                   waits=[("ve", v6)], inc="sX1")
            w1_ = [("ve", v6)]
            u1 = sd.add("vector",
                        (lambda par: lambda eng: eng.bn_stats(
                            bnb[par][:, 0:6], x1s[par][:, 0:512]))(par),
                        waits=w1_, inc="ve")
            u2 = sd.add("vector",
                        (lambda par: lambda eng: eng.bn_stats(
                            bnb[par][:, 6:12], x1s[par][:, 512:1024]))(par),
                        inc="ve")
            u3 = sd.add("vector",
                        (lambda par: lambda eng: eng.bn_aggr(
                            st2[par][:], bnb[par][:, 0:12]))(par),
                        waits=[("ve", u2)], inc="ve")
            t1 = sd.add("scalar",
                        (lambda par: lambda eng: eng.activation(
                            out=rstd[par][:], in_=st2[par][:, 1:2], func=AF.Ln,
                            bias=P["epsas"][:]))(par),
                        waits=[("ve", u3)], inc="sc")
            t2 = sd.add("scalar",
                        (lambda par: lambda eng: eng.activation(
                            out=rstd[par][:], in_=rstd[par][:], func=AF.Exp,
                            scale=-0.5))(par),
                        waits=[("sc", t1)], inc="sc")
            u4 = sd.add("vector",
                        (lambda par: lambda eng: eng.scalar_tensor_tensor(
                            bnc[par][:], st2[par][:, 0:1], -1.0, rstd[par][:],
                            ALU.mult, ALU.mult))(par),
                        waits=[("sc", t2)], inc="ve")
            t3 = sd.add("scalar",
                        (lambda par: lambda eng: eng.activation(
                            out=x2s[par][:], in_=x1s[par][:], func=AF.Identity,
                            scale=rstd[par][:], bias=bnc[par][:]))(par),
                        waits=[("ve", u4), ("sX2", 1)], inc="sc")
            lnfm_done[("sc_d", b, i)] = t3
            sd.dma("sync", x2_scr[ti * 128:(ti + 1) * 128, :], x2s[par][:],
                   waits=[("sc", t3)], inc="sX2")
            tw = [("sc", t3), ("ve", lnfm_done.get(("x2c", b, i - 1), 0))]
            def mkTX(par):
                def f(eng):
                    for dc in range(NDC):
                        mi = eng.transpose(psav[:, dc * 128:(dc + 1) * 128],
                                           x2s[par][:, dc * 128:(dc + 1) * 128],
                                           P["id128fs"][:])
                    return mi
                return f
            peTX = sd.add("tensor", mkTX(par), waits=tw, inc="pe")
            vx = sd.add("vector",
                        (lambda par: lambda eng: eng.tensor_copy(
                            out=x2fT[par][:], in_=psav[:]))(par),
                        waits=[("pe", peTX),
                               ("pe", lnfm_done.get(("peR", b, i - 1), 0))],
                        inc="ve")
            lnfm_done[("x2c", b, i)] = vx
            def mkR(par):
                def f(eng):
                    for dc in range(NDC):
                        mi = eng.matmul(ps0[:, 0:8],
                                        x2fT[par][:, dc * 128:(dc + 1) * 128],
                                        rws[:, dc * E:(dc + 1) * E],
                                        start=(dc == 0), stop=(dc == NDC - 1))
                    return mi
                return f
            peR = sd.add("tensor", mkR(par), waits=[("ve", vx)], inc="pe")
            lnfm_done[("peR", b, i)] = peR
            gpar = (ti // TPG) % 2
            vc = sd.add(
                "vector",
                (lambda par, gpar, i: lambda eng: eng.tensor_copy(
                    out=x2fg[gpar][:].rearrange("p (dc t) -> p dc t", dc=NDC)[
                        :, :, (i % TPG) * 128:(i % TPG + 1) * 128],
                    in_=x2fT[par][:].rearrange("p (dc t) -> p dc t", dc=NDC)))(
                        par, gpar, i),
                waits=[("ve", vx), ("sF", 1)],
                inc="ve")
            if ti % TPG == TPG - 1:
                g = ti // TPG
                x2f_stored[g] = sd.dma(
                    "sync", x2fm_scr[g].rearrange("dc p t -> p dc t"),
                    x2fg[gpar][:].rearrange("p (dc t) -> p dc t", dc=NDC),
                    waits=[("ve", vc)], inc="sF")
            gv1 = sd.add("vector",
                         lambda eng: eng.tensor_copy(out=glog[:], in_=ps0[:, 0:8]),
                         waits=[("pe", peR)], inc="ve")
            gv2 = sd.add("vector",
                         lambda eng: eng.reduce_max(gm1[:], glog[:], axis=AX.X),
                         waits=[("ve", gv1)], inc="ve")
            gv3 = sd.add("vector",
                         lambda eng: eng.tensor_scalar(gmask1[:], glog[:], gm1[:],
                                                       None, ALU.is_equal),
                         waits=[("ve", gv2)], inc="ve")
            gv4 = sd.add("vector",
                         lambda eng: eng.scalar_tensor_tensor(
                             gmask2[:], gmask1[:], -BIG, glog[:], ALU.mult,
                             ALU.add),
                         waits=[("ve", gv3)], inc="ve")
            gv5 = sd.add("vector",
                         lambda eng: eng.reduce_max(gm2[:], gmask2[:], axis=AX.X),
                         waits=[("ve", gv4)], inc="ve")
            gv6 = sd.add("vector",
                         lambda eng: eng.tensor_tensor(gden[:], gm2[:], gm1[:],
                                                       ALU.subtract),
                         waits=[("ve", gv5)], inc="ve")
            gs1 = sd.add("scalar",
                         lambda eng: eng.activation(out=ge2[:], in_=gden[:],
                                                    func=AF.Exp),
                         waits=[("ve", gv6)], inc="sc")
            gv7 = sd.add("vector",
                         lambda eng: eng.tensor_scalar(gden[:], ge2[:], 1.0,
                                                       None, ALU.add),
                         waits=[("sc", gs1)], inc="ve")
            gv8 = sd.add("vector",
                         lambda eng: eng.reciprocal(out=gden[:], in_=gden[:]),
                         waits=[("ve", gv7)], inc="ve")
            gv9 = sd.add("vector",
                         lambda eng: eng.tensor_tensor(ge2[:], ge2[:], gden[:],
                                                       ALU.mult),
                         waits=[("ve", gv8)], inc="ve")
            gv10 = sd.add("vector",
                          lambda eng: eng.tensor_scalar(g1c[:], ge2[:], -1.0,
                                                        1.0, ALU.mult, ALU.add),
                          waits=[("ve", gv9)], inc="ve")
            gv11 = sd.add("vector",
                          lambda eng: eng.tensor_scalar(gmask2[:], glog[:],
                                                        gm2[:], None,
                                                        ALU.is_equal),
                          waits=[("ve", gv9)], inc="ve")
            gv12 = sd.add("vector",
                          lambda eng: eng.tensor_scalar(gmask1[:], gmask1[:],
                                                        g1c[:], None, ALU.mult),
                          waits=[("ve", gv10)], inc="ve")
            gv13 = sd.add("vector",
                          lambda eng: eng.tensor_scalar(gmask2[:], gmask2[:],
                                                        ge2[:], None, ALU.mult),
                          waits=[("ve", gv11)], inc="ve")
            gv14 = sd.add("vector",
                          (lambda ti: lambda eng: eng.tensor_tensor(
                              P["gates_sb"][:, ti * E:(ti + 1) * E], gmask1[:],
                              gmask2[:], ALU.add))(ti),
                          waits=[("ve", gv13), ("ve", gv12)], inc="ve")
            peG = sd.add("tensor",
                         (lambda ti: lambda eng: eng.transpose(
                             ps0[0:8, 128:256],
                             P["gates_sb"][:, ti * E:(ti + 1) * E],
                             P["id128fs"][:]))(ti),
                         waits=[("ve", gv14)], inc="pe")
            gv15 = sd.add("vector",
                          (lambda par: lambda eng: eng.tensor_copy(
                              out=gfmst[par][:], in_=ps0[0:8, 128:256]))(par),
                          waits=[("pe", peG), ("sG", 1)],
                          inc="ve")
            gfm_used2[ti] = sd.dma(
                "sync", gatesfm_scr[:, ti * 128:(ti + 1) * 128], gfmst[par][:],
                waits=[("ve", gv15)], inc="sG")
        batch_prev_ve = sd.now("ve")

    ast.close()
    return sd.barrier_counts()


def _prep_inputs(inputs):
    x = np.ascontiguousarray(np.asarray(inputs["L0_attn_x"], np.float32))
    Wqkv = np.asarray(inputs["L0_attn_W_qkv"], np.float32)
    bqkv = np.asarray(inputs["L0_attn_b_qkv"], np.float32)
    Wout = np.asarray(inputs["L0_attn_W_out"], np.float32)
    bout = np.asarray(inputs["L0_attn_b_out"], np.float32)

    perm = []
    for comp in range(3):
        for h in range(H):
            perm.extend(range(h * 192 + comp * 64, h * 192 + comp * 64 + 64))
    perm = np.array(perm)
    WqkvT = np.ascontiguousarray(Wqkv[perm].T)
    bqkv_cols = np.ascontiguousarray(bqkv[perm].reshape(NJT, 128).T)
    rows8 = np.zeros((E, C), np.float32)
    rows8[0] = bout
    b1 = np.asarray(inputs["L0_ffn_b1"], np.float32)
    oh8 = np.zeros((E, E * 128), np.float32)
    for e in range(E):
        oh8[e, e * 128:(e + 1) * 128] = 1.0

    common = {
        "wqkvT": WqkvT,
        "bqkv": bqkv_cols,
        "woutT": np.ascontiguousarray(Wout.T),
        "rows8": rows8,
        "rw": np.ascontiguousarray(np.asarray(inputs["L0_ffn_router_w"], np.float32)),
        "w1": np.ascontiguousarray(np.asarray(inputs["L0_ffn_w1"], np.float32)),
        "b1c": np.ascontiguousarray(
            b1.reshape(E, NFC, 128).transpose(2, 0, 1).reshape(128, E * NFC)),
        "w2": np.ascontiguousarray(np.asarray(inputs["L0_ffn_w2"], np.float32)),
        "b2": np.ascontiguousarray(np.asarray(inputs["L0_ffn_b2"], np.float32)),
        "gam": np.ascontiguousarray(np.asarray(inputs["L0_ffn_gamma"], np.float32)),
        "bet": np.ascontiguousarray(np.asarray(inputs["L0_ffn_beta"], np.float32)),
        "allones": np.ones((128, 128), np.float32),
        "ones_col": np.ones((128, 1), np.float32),
        "oh8f": oh8,
        "oh8b": oh8.astype(ml_dtypes.bfloat16),
        "id128b": np.eye(128, dtype=ml_dtypes.bfloat16),
        "id128f": np.eye(128, dtype=np.float32),
        "id64sb": np.vstack([np.eye(64), np.eye(64)]).astype(ml_dtypes.bfloat16),
        "epsa": np.full((128, 1), EPS_A, np.float32),
        "epsm": np.full((128, 1), EPS_M, np.float32),
    }
    shards = []
    for c in range(NCORES):
        m = dict(common)
        m["x"] = np.ascontiguousarray(x[c * BPC:(c + 1) * BPC].reshape(NTOK, C))
        shards.append(m)
    return shards


def kernel(**inputs):
    shards = _prep_inputs(inputs)
    nc = build_nc(dev_moe_only=False)
    res = run_bass_kernel_spmd(nc, shards, core_ids=list(range(NCORES)))
    outs = [r["out"].reshape(BPC, T, C) for r in res.results]
    return np.concatenate(outs, axis=0)


# version 05
# speedup vs baseline: 1.2164x; 1.2164x over previous
import contextlib
import numpy as np
import ml_dtypes
import concourse.bass as bass
import concourse.mybir as mybir
from concourse.bass_utils import run_bass_kernel_spmd

F32 = mybir.dt.float32
BF16 = mybir.dt.bfloat16
U8 = mybir.dt.uint8
AF = mybir.ActivationFunctionType
ALU = mybir.AluOpType
AX = mybir.AxisListType

B, T, C = 16, 2048, 1024
H, HS = 16, 64
E, FF = 8, 4096
NCORES = 8
BPC = B // NCORES
NTOK = BPC * T
NDC = C // 128
NFC = FF // 128
NJT = 3 * C // 128
NTILE = NTOK // 128
GT = 256
NG = NTOK // GT
TPG = GT // 128
EPS_A = 1e-5
EPS_M = 1e-6
BIG = 1.0e30


class Sched:
    ENG = ("tensor", "vector", "scalar", "sync", "gpsimd")

    def __init__(self):
        self.ops = {e: [] for e in self.ENG}
        self.cnt = {}
        self.dma_sems = set()

    def sem(self, name, dma=False):
        self.cnt.setdefault(name, 0)
        if dma:
            self.dma_sems.add(name)

    def now(self, name):
        return self.cnt[name]

    def add(self, eng, fn, waits=(), inc=None, amt=1):
        fixed = []
        for (s, v) in waits:
            if s in self.dma_sems:
                v = self.cnt[s]
            fixed.append((s, v))
        v = None
        if inc is not None:
            self.cnt[inc] += amt
            v = self.cnt[inc]
        self.ops[eng].append((fn, tuple(fixed), inc, amt))
        return v

    def dma(self, queue, out, in_, waits=(), inc=None):
        if inc is not None:
            self.sem(inc, dma=True)
        def fn(eng, out=out, in_=in_):
            return eng.dma_start(out=out, in_=in_)
        return self.add(queue, fn, waits=waits, inc=inc, amt=16)

    def barrier_counts(self):
        return dict(self.cnt)

    def emit(self, blk, sems, final=None):
        for name in self.ENG:
            items = list(self.ops[name])
            if not items and not (final and final[0] == name):
                continue

            def mk(items, name):
                def body(eng):
                    hwm = {}
                    for fn, waits, inc, amt in items:
                        for (s, v) in waits:
                            if v > 0 and hwm.get(s, -1) < v:
                                eng.wait_ge(sems[s], v)
                                hwm[s] = v
                        inst = fn(eng)
                        if inc is not None:
                            inst.then_inc(sems[inc], amt)
                    if final and final[0] == name:
                        eng.wait_ge(sems[final[1]], final[2])
                return body
            getattr(blk, name)(mk(items, name))


def build_nc(dev_moe_only=False, dbg_scratch=False):
    nc = bass.Bass("TRN2", target_bir_lowering=False, debug=False)

    dr = {}
    def din(name, shape, dtype=F32):
        dr[name] = nc.dram_tensor(name, list(shape), dtype, kind="ExternalInput").ap()
        return dr[name]

    din("x", (NTOK, C))
    din("wqkvT", (C, 3 * C))
    din("bqkv", (128, NJT))
    din("woutT", (C, C))
    din("rows8", (E, C))
    din("rw", (C, E))
    din("w1", (E, C, FF))
    din("b1c", (128, E * NFC))
    din("w2", (E, FF, C))
    din("b2", (E, C))
    din("gam", (E, C))
    din("bet", (E, C))
    din("allones", (128, 128))
    din("ones_col", (128, 1))
    din("oh8f", (E, E * 128))
    din("oh8b", (E, E * 128), BF16)
    din("id128b", (128, 128), BF16)
    din("id128f", (128, 128))
    din("id64sb", (128, 64), BF16)
    din("epsa", (128, 1))
    din("epsm", (128, 1))

    out_t = nc.dram_tensor("out", [NTOK, C], F32, kind="ExternalOutput").ap()

    skind = "ExternalInput" if dev_moe_only else (
        "ExternalOutput" if dbg_scratch else "Internal")
    x1_scr = nc.dram_tensor("x1_scr", [NTOK, C], F32, kind=skind).ap()
    x2_scr = nc.dram_tensor("x2_scr", [NTOK, C], F32, kind=skind).ap()
    x2fm_scr = nc.dram_tensor("x2fm_scr", [NG, NDC, 128, GT], BF16, kind=skind).ap()
    gates_scr = nc.dram_tensor("gates_scr", [NTILE, 128, E], F32, kind=skind).ap()
    gatesfm_scr = nc.dram_tensor("gatesfm_scr", [E, NTOK], F32, kind=skind).ap()
    y_scr = nc.dram_tensor("y_scr", [NTOK, C], F32, kind="Internal").ap()
    w1b_scr = nc.dram_tensor("w1b_scr", [E, C, FF], BF16, kind="Internal").ap()
    w2b_scr = nc.dram_tensor("w2b_scr", [E, FF, C], BF16, kind="Internal").ap()

    sd = Sched()
    for s in ("pe", "ve", "sc"):
        sd.sem(s)
    for s in ("lc", "wc", "lx0", "lx1", "lq0", "lq1", "sA", "lA0", "lA1",
              "sX1", "sX2", "sF", "sG", "sY", "ly", "lt0", "lt1",
              "lf0", "lf1", "wload", "lgf0", "lgf1", "sO", "wcv"):
        sd.sem(s, dma=True)

    with contextlib.ExitStack() as est:
        P = {}
        def palloc(name, shape, dtype=F32):
            P[name] = est.enter_context(nc.sbuf_tensor(name, list(shape), dtype))
            return P[name]
        palloc("ones_fs", (128, 128))
        palloc("ones_cs", (128, 1))
        palloc("oh8fs", (E, E * 128))
        palloc("oh8bs", (E, E * 128), BF16)
        palloc("id128bs", (128, 128), BF16)
        palloc("id128fs", (128, 128))
        palloc("id64sbs", (128, 64), BF16)
        palloc("epsas", (128, 1))
        palloc("epsms", (128, 1))
        palloc("gates_sb", (128, NTILE * E))
        palloc("b2s", (E, C), BF16)
        palloc("bets", (E, C), BF16)
        palloc("gamrows", (E, C))
        palloc("b1s", (128, E * NFC))

        for nm, src in (("ones_fs", "allones"), ("ones_cs", "ones_col"),
                        ("oh8fs", "oh8f"), ("id128bs", "id128b"),
                        ("id128fs", "id128f"), ("id64sbs", "id64sb"),
                        ("epsas", "epsa"), ("epsms", "epsm"),
                        ("gamrows", "gam")):
            sd.dma("sync", P[nm][:], dr[src], inc="lc")
        sd.dma("sync", P["b1s"][:], dr["b1c"], inc="lc")
        for nm, src in (("oh8bs", "oh8b"), ("b2s", "b2"), ("bets", "bet")):
            sd.dma("gpsimd", P[nm][:], dr[src], inc="wc")
        ld_consts = sd.now("lc")
        wl_consts = sd.now("wc")

        if dev_moe_only:
            dr["w1b_scr"] = w1b_scr
            dr["w2b_scr"] = w2b_scr
            for ci in range(16):
                e, hf = ci // 2, ci % 2
                sd.dma("gpsimd", w1b_scr[e, hf * 512:(hf + 1) * 512, :],
                       dr["w1"][e, hf * 512:(hf + 1) * 512, :], inc="wcv")
            for ci in range(16):
                e, hf = ci // 2, ci % 2
                sd.dma("gpsimd", w2b_scr[e, hf * 2048:(hf + 1) * 2048, :],
                       dr["w2"][e, hf * 2048:(hf + 1) * 2048, :], inc="wcv")
            for ti in range(NTILE):
                sd.dma("sync", P["gates_sb"][:, ti * E:(ti + 1) * E], gates_scr[ti],
                       inc="lc")
            barrier = sd.barrier_counts()
        else:
            av_scr = nc.dram_tensor("av_scr", [NDC, 128, T], BF16,
                                    kind=("ExternalOutput" if dbg_scratch
                                          else "Internal")).ap()
            dr["w1b_scr"] = w1b_scr
            dr["w2b_scr"] = w2b_scr
            barrier = build_attention(nc, sd, est, dr, P, x1_scr, x2_scr,
                                      x2fm_scr, gates_scr, gatesfm_scr, av_scr,
                                      ld_consts, wl_consts)

        build_moe(nc, sd, est, dr, P, x1_scr, x2_scr, x2fm_scr, y_scr, out_t,
                  barrier, ld_consts, wl_consts, gatesfm_scr)

        sems = {}
        for name in sd.cnt:
            sems[name] = est.enter_context(nc.semaphore(name))
        blk = est.enter_context(nc.Block("main"))
        sd.emit(blk, sems, final=("sync", "sO", sd.now("sO")))
    return nc


def build_moe(nc, sd, est, dr, P, x1_scr, x2_scr, x2fm_scr, y_scr, out_t,
              barrier, ld_consts, wl_consts, gatesfm_scr):
    moe_st = est.enter_context(contextlib.ExitStack())
    A = {}
    def alloc(name, shape, dtype=F32):
        A[name] = moe_st.enter_context(nc.sbuf_tensor(name, list(shape), dtype))
        return A[name]
    w1s = [alloc(f"w1s{i}", (128, FF), BF16) for i in range(NDC)]
    w2s = [alloc(f"w2s{i}", (128, C), BF16) for i in range(NFC)]
    hsb = [alloc(f"hsb{i}", (128, GT), BF16) for i in range(NFC)]
    x2f = [alloc(f"x2f{i}", (128, NDC * GT), BF16) for i in range(2)]
    x2t = [alloc(f"x2t{i}", (128, C)) for i in range(2)]
    ysb = [alloc(f"ysb{i}", (128, C)) for i in range(2)]
    res = [alloc(f"res{i}", (128, C)) for i in range(2)]
    gfm = [alloc(f"gfm{i}", (E, 128), BF16) for i in range(2)]
    gamr = alloc("gamr", (128, C))
    bnb = [alloc(f"bnb{i}", (128, 12)) for i in range(2)]
    st2 = [alloc(f"st2{i}", (128, 2)) for i in range(2)]
    rstd = [alloc(f"rstd{i}", (128, 1)) for i in range(2)]
    sgc = [alloc(f"sgc{i}", (128, 1)) for i in range(2)]
    bnc = [alloc(f"bnc{i}", (128, 1)) for i in range(2)]
    hp = [moe_st.enter_context(nc.psum_tensor(f"hp{i}", [128, 512], F32))
          for i in range(2)]
    op = [moe_st.enter_context(nc.psum_tensor(f"op{i}", [128, C], F32))
          for i in range(2)]
    mp = moe_st.enter_context(nc.psum_tensor("mp", [128, C], F32))

    bw = lambda s: [(s, barrier.get(s, 0))]
    dma_bar = [(s, 1) for s in sorted(sd.dma_sems)]
    bar_all = [(s, barrier.get(s, 0)) for s in ("pe", "ve", "sc")] + dma_bar

    NQ = 4
    w1_rd = [[0] * NDC for _ in range(E)]
    w2_rd = [[0] * NFC for _ in range(E)]
    relA = [0] * E
    relB = [[0] * NQ for _ in range(E)]
    pe_after_A = {}
    gelu_done = {}
    res_done = {}
    scn_done = {}
    out_done = {}
    gfm_used = {}
    y_store = {}
    gam_ready = [0] * E
    pending_fin = [None]

    def plan_finalize(e, g, peBs):
        for tt in range(TPG):
            ti = g * TPG + tt
            bi = (e * NG + g) * TPG + tt
            par = bi % 2
            peB = peBs[tt]
            xw = []
            if bi >= 2:
                xw.append(("ve", res_done[bi - 2]))
                if e == E - 1 and ti >= 2:
                    xw.append(("ve", out_done[ti - 2]))
            elif bi == 0:
                xw += bw("ve") + dma_bar
            x2ok = sd.dma("sync", x2t[par][:],
                          x2_scr[ti * 128:(ti + 1) * 128, :],
                          waits=xw + [("sX2", 1)], inc="lt%d" % par)
            if e > 0:
                yv = max(y_store.get((e, ti - 2), 0),
                         y_store.get((e - 1, ti), 0))
                yok = sd.dma("sync", ysb[par][:],
                             y_scr[ti * 128:(ti + 1) * 128, :],
                             waits=[("sY", yv), ("sO", 1)], inc="ly")
            else:
                yok = 0
            rw_ = [("pe", peB), ("lt%d" % par, x2ok)]
            if bi >= 2:
                rw_.append(("sc", scn_done[bi - 2]))
            ve_res = sd.add(
                "vector",
                (lambda par: lambda eng: eng.tensor_tensor(
                    res[par][:], op[par][:], x2t[par][:], ALU.add))(par),
                waits=rw_, inc="ve")
            res_done[bi] = ve_res
            sd.add("vector",
                   (lambda par: lambda eng: eng.bn_stats(
                       bnb[par][:, 0:6], res[par][:, 0:512]))(par),
                   waits=[("ve", ve_res)], inc="ve")
            ve_s2 = sd.add("vector",
                           (lambda par: lambda eng: eng.bn_stats(
                               bnb[par][:, 6:12], res[par][:, 512:1024]))(par),
                           inc="ve")
            ve_bn = sd.add("vector",
                           (lambda par: lambda eng: eng.bn_aggr(
                               st2[par][:], bnb[par][:, 0:12]))(par),
                           waits=[("ve", ve_s2)], inc="ve")
            sc_l = sd.add(
                "scalar",
                (lambda par: lambda eng: eng.activation(
                    out=rstd[par][:], in_=st2[par][:, 1:2],
                    func=AF.Ln, bias=P["epsms"][:]))(par),
                waits=[("ve", ve_bn)], inc="sc")
            sc_r = sd.add(
                "scalar",
                (lambda par: lambda eng: eng.activation(
                    out=rstd[par][:], in_=rstd[par][:],
                    func=AF.Exp, scale=-0.5))(par),
                waits=[("sc", sc_l)], inc="sc")
            ve_g1 = sd.add(
                "vector",
                (lambda e, ti, par: lambda eng: eng.tensor_tensor(
                    sgc[par][:], rstd[par][:],
                    P["gates_sb"][:, ti * E + e:ti * E + e + 1], ALU.mult))(e, ti, par),
                waits=[("sc", sc_r)], inc="ve")
            ve_sg = sd.add(
                "vector",
                (lambda par: lambda eng: eng.scalar_tensor_tensor(
                    bnc[par][:], st2[par][:, 0:1], -1.0, sgc[par][:],
                    ALU.mult, ALU.mult))(par),
                waits=[("ve", ve_g1)], inc="ve")
            sc_n = sd.add(
                "scalar",
                (lambda par: lambda eng: eng.activation(
                    out=res[par][:], in_=res[par][:], func=AF.Identity,
                    scale=sgc[par][:], bias=bnc[par][:]))(par),
                waits=[("ve", ve_sg)], inc="sc")
            scn_done[bi] = sc_n

            if e == 0:
                ve_y = sd.add(
                    "vector",
                    (lambda par: lambda eng: eng.tensor_tensor(
                        ysb[par][:], res[par][:], gamr[:], ALU.mult))(par),
                    waits=[("sc", sc_n), ("ve", gam_ready[e]), ("sY", 1)],
                    inc="ve")
            else:
                ve_t = sd.add(
                    "vector",
                    (lambda par: lambda eng: eng.tensor_tensor(
                        res[par][:], res[par][:], gamr[:], ALU.mult))(par),
                    waits=[("sc", sc_n), ("ve", gam_ready[e])], inc="ve")
                ve_y = sd.add(
                    "vector",
                    (lambda par: lambda eng: eng.tensor_tensor(
                        ysb[par][:], ysb[par][:], res[par][:], ALU.add))(par),
                    waits=[("ve", ve_t), ("ly", yok)], inc="ve")

            if e < E - 1:
                y_store[(e, ti)] = sd.dma(
                    "sync", y_scr[ti * 128:(ti + 1) * 128, :], ysb[par][:],
                    waits=[("ve", ve_y)], inc="sY")
            else:
                gfmok = sd.dma("gpsimd", gfm[par][:],
                               gatesfm_scr[:, ti * 128:(ti + 1) * 128],
                               waits=[("pe", gfm_used.get(ti - 2, 0)),
                                      ("sG", 1)],
                               inc="lgf%d" % par)
                def mkBeta(ti, par):
                    def f(eng):
                        for dh in range(2):
                            mi = eng.matmul(
                                mp[:, dh * 512:(dh + 1) * 512],
                                gfm[par][:],
                                P["bets"][:, dh * 512:(dh + 1) * 512],
                                start=True, stop=True)
                        return mi
                    return f
                pe_bg = sd.add("tensor", mkBeta(ti, par),
                               waits=[("ve", ve_y), ("wc", wl_consts),
                                      ("lgf%d" % par, gfmok)], inc="pe")
                gfm_used[ti] = pe_bg
                ve_yt = sd.add(
                    "vector",
                    (lambda par: lambda eng: eng.tensor_tensor(
                        ysb[par][:], ysb[par][:], mp[:], ALU.add))(par),
                    waits=[("pe", pe_bg)], inc="ve")
                sd.add("vector",
                       (lambda par: lambda eng: eng.bn_stats(
                           bnb[par][:, 0:6], ysb[par][:, 0:512]))(par),
                       waits=[("ve", ve_yt)], inc="ve")
                ve_q2 = sd.add("vector",
                               (lambda par: lambda eng: eng.bn_stats(
                                   bnb[par][:, 6:12], ysb[par][:, 512:1024]))(par),
                               inc="ve")
                ve_bn2 = sd.add("vector",
                                (lambda par: lambda eng: eng.bn_aggr(
                                    st2[par][:], bnb[par][:, 0:12]))(par),
                                waits=[("ve", ve_q2)], inc="ve")
                sc_l2 = sd.add(
                    "scalar",
                    (lambda par: lambda eng: eng.activation(
                        out=rstd[par][:], in_=st2[par][:, 1:2],
                        func=AF.Ln, bias=P["epsas"][:]))(par),
                    waits=[("ve", ve_bn2)], inc="sc")
                sc_r2 = sd.add(
                    "scalar",
                    (lambda par: lambda eng: eng.activation(
                        out=rstd[par][:], in_=rstd[par][:],
                        func=AF.Exp, scale=-0.5))(par),
                    waits=[("sc", sc_l2)], inc="sc")
                ve_b2 = sd.add(
                    "vector",
                    (lambda par: lambda eng: eng.scalar_tensor_tensor(
                        bnc[par][:], st2[par][:, 0:1], -1.0, rstd[par][:],
                        ALU.mult, ALU.mult))(par),
                    waits=[("sc", sc_r2)], inc="ve")
                sc_n2 = sd.add(
                    "scalar",
                    (lambda par: lambda eng: eng.activation(
                        out=res[par][:], in_=ysb[par][:], func=AF.Identity,
                        scale=rstd[par][:], bias=bnc[par][:]))(par),
                    waits=[("ve", ve_b2)], inc="sc")
                scn_done[bi] = sc_n2
                x1ok = sd.dma("sync", x2t[par][:],
                              x1_scr[ti * 128:(ti + 1) * 128, :],
                              waits=[("ve", res_done[bi]), ("sX1", 1)],
                              inc="lt%d" % par)
                ve_out = sd.add(
                    "vector",
                    (lambda par: lambda eng: eng.tensor_tensor(
                        ysb[par][:], res[par][:], x2t[par][:], ALU.add))(par),
                    waits=[("sc", sc_n2), ("lt%d" % par, x1ok)], inc="ve")
                out_done[ti] = ve_out
                y_store[(e, ti)] = sd.dma(
                    "sync", out_t[ti * 128:(ti + 1) * 128, :], ysb[par][:],
                    waits=[("ve", ve_out)], inc="sO")

    for e in range(E):
        for dc in range(NDC):
            w = (bar_all + [("wcv", 1)]) if (e == 0 and dc == 0) else \
                ([("pe", relA[e - 1]), ("wcv", 1)] if (e > 0 and dc == 0) else ())
            w1_rd[e][dc] = sd.dma("gpsimd", w1s[dc][:],
                                  dr["w1b_scr"][e, dc * 128:(dc + 1) * 128, :],
                                  waits=w, inc="wload")
        for fc in range(NFC):
            q = fc // (NFC // NQ)
            w = [("pe", relB[e - 1][q]), ("wcv", 1)] \
                if (e > 0 and fc % (NFC // NQ) == 0) else ()
            w2_rd[e][fc] = sd.dma("gpsimd", w2s[fc][:],
                                  dr["w2b_scr"][e, fc * 128:(fc + 1) * 128, :],
                                  waits=w, inc="wload")

        for g in range(NG):
            gi = e * NG + g
            xw = list(dma_bar) if gi == 0 else []
            if gi >= 2:
                xw.append(("pe", pe_after_A[gi - 2]))
            x2f_ok = sd.dma("sync",
                            x2f[gi % 2][:].rearrange("p (dc t) -> p dc t", dc=NDC),
                            x2fm_scr[g].rearrange("dc p t -> p dc t"),
                            waits=xw, inc="lf%d" % (gi % 2))

            for fc in range(NFC):
                it = gi * NFC + fc
                waits = [("wload", w1_rd[e][NDC - 1]),
                         ("lf%d" % (gi % 2), x2f_ok)]
                if it >= 2:
                    waits.append(("sc", gelu_done[it - 2]))
                def mkA(gi, fc):
                    def f(eng):
                        for dc in range(NDC):
                            mi = eng.matmul(
                                hp[fc % 2][:, 0:GT],
                                w1s[dc][:, fc * 128:(fc + 1) * 128],
                                x2f[gi % 2][:, dc * GT:(dc + 1) * GT],
                                start=(dc == 0), stop=(dc == NDC - 1))
                        return mi
                    return f
                peA = sd.add("tensor", mkA(gi, fc), waits=waits, inc="pe")
                gelu_done[it] = sd.add(
                    "scalar",
                    (lambda e, fc: lambda eng: eng.activation(
                        out=hsb[fc][:], in_=hp[fc % 2][:, 0:GT], func=AF.Gelu,
                        bias=P["b1s"][:, e * NFC + fc:e * NFC + fc + 1]))(e, fc),
                    waits=[("pe", peA)], inc="sc")
            pe_after_A[gi] = sd.now("pe")
            if g == NG - 1:
                relA[e] = pe_after_A[gi]

            if pending_fin[0] is not None:
                pending_fin[0]()
                pending_fin[0] = None

            if g == 0:
                gw = [("wc", wl_consts), ("lc", ld_consts), ("ve", sd.now("ve"))]
                if e == 0:
                    gw += bar_all
                def mk_g(e):
                    def f(eng):
                        for dh in range(2):
                            mi = eng.matmul(mp[:, dh * 512:(dh + 1) * 512],
                                            P["oh8fs"][:, e * 128:(e + 1) * 128],
                                            P["gamrows"][:, dh * 512:(dh + 1) * 512],
                                            start=True, stop=True)
                        return mi
                    return f
                pe_g = sd.add("tensor", mk_g(e), waits=gw, inc="pe")
                gam_ready[e] = sd.add(
                    "vector", lambda eng: eng.tensor_copy(out=gamr[:], in_=mp[:]),
                    waits=[("pe", pe_g)], inc="ve")

            peBs = []
            last_grp = (g == NG - 1)
            for tt in range(TPG):
                ti = g * TPG + tt
                bi = gi * TPG + tt
                par = bi % 2
                waits = [("sc", gelu_done[gi * NFC + NFC - 1]),
                         ("wload", w2_rd[e][NFC - 1])]
                if bi >= 2:
                    waits.append(("ve", res_done[bi - 2]))
                mark_q = last_grp and tt == TPG - 1
                qsz = NFC // NQ
                for q in range(NQ):
                    def mkB(e, par, tt, q, qsz):
                        def f(eng):
                            for fc in range(q * qsz, (q + 1) * qsz):
                                for dh in range(2):
                                    mi = eng.matmul(
                                        op[par][:, dh * 512:(dh + 1) * 512],
                                        hsb[fc][:, tt * 128:(tt + 1) * 128],
                                        w2s[fc][:, dh * 512:(dh + 1) * 512],
                                        start=(fc == 0), stop=False)
                            if q == NQ - 1:
                                for dh in range(2):
                                    mi = eng.matmul(
                                        op[par][:, dh * 512:(dh + 1) * 512],
                                        P["oh8bs"][:, e * 128:(e + 1) * 128],
                                        P["b2s"][:, dh * 512:(dh + 1) * 512],
                                        start=False, stop=True)
                            return mi
                        return f
                    peB = sd.add("tensor", mkB(e, par, tt, q, qsz),
                                 waits=waits if q == 0 else (), inc="pe")
                    if mark_q:
                        relB[e][q] = peB
                peBs.append(peB)
            pending_fin[0] = (lambda e, g, peBs: lambda: plan_finalize(e, g, peBs))(e, g, peBs)

    pending_fin[0]()
    pending_fin[0] = None


DBG = [False]


def build_attention(nc, sd, est, dr, P, x1_scr, x2_scr, x2fm_scr,
                    gates_out_scr, gatesfm_scr, av_scr, ld_consts, wl_consts):
    ast = contextlib.ExitStack()
    A = {}
    def alloc(name, shape, dtype=F32):
        A[name] = ast.enter_context(nc.sbuf_tensor(name, list(shape), dtype))
        return A[name]

    kv = alloc("kv", (128, 16 * T), BF16)
    qring = [alloc(f"qring{i}", (128, T), BF16) for i in range(2)]
    lnfm = alloc("lnfm", (128, NDC * T), BF16)
    woT = alloc("woT", (128, NDC * C), BF16)
    wqb = [alloc(f"wqb{i}", (128, NDC * 128), BF16) for i in range(2)]
    vh = alloc("vh", (128, 16 * 65), BF16)
    pT2 = alloc("pT2", (128, 2048), BF16)
    q2b = pT2
    avb = [alloc(f"avb{i}", (128, NDC * 128), BF16) for i in range(2)]
    xres = [alloc(f"xres{i}", (128, C)) for i in range(2)]
    x1s = [alloc(f"x1s{i}", (128, C)) for i in range(2)]
    x2s = [alloc(f"x2s{i}", (128, C)) for i in range(2)]
    lnb = [alloc(f"lnb{i}", (128, C), BF16) for i in range(2)]
    avst = lnb
    sumbc = alloc("sumbc", (128, 1024))
    x2fT0 = alloc("x2fT0", (128, C))
    x2fT = [x2fT0, x2fT0]
    prj = [alloc(f"prj{i}", (128, C)) for i in range(2)]
    x2fg = [alloc(f"x2fg{i}", (128, NDC * GT), BF16) for i in range(2)]
    rws = alloc("rws", (128, NDC * E))
    rows8b = alloc("rows8b", (E, C), BF16)
    bqkvs = alloc("bqkvs", (128, NJT))
    gfmst = [alloc(f"gfmst{i}", (E, 128)) for i in range(2)]
    bnb = [alloc(f"abnb{i}", (128, 12)) for i in range(2)]
    st2 = [alloc(f"ast2{i}", (128, 2)) for i in range(2)]
    rstd = [alloc(f"arstd{i}", (128, 1)) for i in range(2)]
    bnc = [alloc(f"abnc{i}", (128, 1)) for i in range(2)]
    rm = alloc("rm", (128, 2))
    csb = alloc("csb", (1, 1))
    ebias = [alloc(f"ebias{i}", (128, 1)) for i in range(2)]
    glog = alloc("glog", (128, E))
    gm1 = alloc("gm1", (128, 1))
    gm2 = alloc("gm2", (128, 1))
    gmask1 = alloc("gmask1", (128, E))
    gmask2 = alloc("gmask2", (128, E))
    ge2 = alloc("ge2", (128, 1))
    gden = alloc("gden", (128, 1))
    g1c = alloc("g1c", (128, 1))

    ps01 = ast.enter_context(nc.psum_tensor("aps01", [128, 2048], F32))
    ps0 = ps01[:, 0:1024]
    ps1 = ps01[:, 1024:2048]
    psav = ast.enter_context(nc.psum_tensor("apsav", [128, 1024], F32))
    pv = ast.enter_context(nc.psum_tensor("apv", [128, 1024], BF16))

    ONES = P["ones_fs"]
    dbg = nc.dram_tensor("dbg_kv", [2, 128, 16 * T], BF16,
                         kind="ExternalOutput").ap() if DBG[0] else None

    sd.dma("gpsimd", woT[:].rearrange("p (dc c) -> p dc c", dc=NDC),
           dr["woutT"].rearrange("(dc p) c -> p dc c", p=128), inc="wc")
    sd.dma("gpsimd", rows8b[:], dr["rows8"], inc="wc")
    sd.dma("sync", bqkvs[:], dr["bqkv"], inc="lc")
    sd.dma("sync", rws[:].rearrange("p (dc e) -> p dc e", dc=NDC),
           dr["rw"].rearrange("(dc p) e -> p dc e", p=128), inc="lc")
    vh_set = sd.add("vector", lambda eng: eng.memset(vh[:], 1.0), inc="ve")

    lnfm3 = lnfm[:].rearrange("p (dc t) -> p dc t", dc=NDC)

    wcv_chunks = []
    for e in range(E):
        for hf in range(2):
            wcv_chunks.append((1, e, hf))
            wcv_chunks.append((2, e, hf))

    lnfm_done = {}
    head_exp_end = {}
    vm_done = {}
    batch_prev_ve = 0
    evacA = {}
    qkv_evac = {}
    wq_used = {}
    pv_free = {}
    head_pe_end = {}
    head_sc_end = {}
    avdma = {}
    va_done = {}
    x2f_stored = {}
    gfm_used2 = {}
    proj_end_pe = 0
    pvi = 0
    peT_done = {}

    for b in range(BPC):
        xoff = b * T

        for i in range(T // 128):
            par = i % 2
            xw = [("sc", evacA.get((b, i - 2), 0))]
            xok = sd.dma("sync", xres[par][:],
                         dr["x"][xoff + i * 128: xoff + (i + 1) * 128, :],
                         waits=xw, inc="lx%d" % par)
            v1 = sd.add("vector",
                        (lambda par: lambda eng: eng.bn_stats(
                            bnb[par][:, 0:6], xres[par][:, 0:512]))(par),
                        waits=[("lx%d" % par, xok),
                               ("sc", evacA.get((b, i - 2), 0))],
                        inc="ve")
            v2 = sd.add("vector",
                        (lambda par: lambda eng: eng.bn_stats(
                            bnb[par][:, 6:12], xres[par][:, 512:1024]))(par),
                        inc="ve")
            v3 = sd.add("vector",
                        (lambda par: lambda eng: eng.bn_aggr(
                            st2[par][:], bnb[par][:, 0:12]))(par),
                        waits=[("ve", v2)], inc="ve")
            s1 = sd.add("scalar",
                        (lambda par: lambda eng: eng.activation(
                            out=rstd[par][:], in_=st2[par][:, 1:2], func=AF.Ln,
                            bias=P["epsas"][:]))(par),
                        waits=[("ve", v3)], inc="sc")
            s2 = sd.add("scalar",
                        (lambda par: lambda eng: eng.activation(
                            out=rstd[par][:], in_=rstd[par][:], func=AF.Exp,
                            scale=-0.5))(par),
                        waits=[("sc", s1)], inc="sc")
            v4 = sd.add("vector",
                        (lambda par: lambda eng: eng.scalar_tensor_tensor(
                            bnc[par][:], st2[par][:, 0:1], -1.0, rstd[par][:],
                            ALU.mult, ALU.mult))(par),
                        waits=[("sc", s2)], inc="ve")
            ew = [("ve", v4), ("pe", peT_done.get((b, i - 2), 0)), ("sA", 1)]
            s3 = sd.add("scalar",
                        (lambda par: lambda eng: eng.activation(
                            out=lnb[par][:], in_=xres[par][:], func=AF.Identity,
                            scale=rstd[par][:], bias=bnc[par][:]))(par),
                        waits=ew, inc="sc")
            evacA[(b, i)] = s3
            tw = [("sc", s3), ("ve", lnfm_done.get((b, i - 1), vh_set))]
            if b > 0 and i == 0:
                tw.append(("pe", proj_end_pe))
            def mkT(par):
                def f(eng):
                    for dc in range(NDC):
                        mi = eng.transpose(pv[:, dc * 128:(dc + 1) * 128],
                                           lnb[par][:, dc * 128:(dc + 1) * 128],
                                           P["id128bs"][:])
                    return mi
                return f
            peT = sd.add("tensor", mkT(par), waits=tw, inc="pe")
            peT_done[(b, i)] = peT
            lnfm_done[(b, i)] = sd.add(
                "vector",
                (lambda i: lambda eng: eng.tensor_copy(
                    out=lnfm3[:, :, i * 128:(i + 1) * 128],
                    in_=pv[:].rearrange("p (dc t) -> p dc t", dc=NDC)))(i),
                waits=[("pe", peT)], inc="ve")
        lnfm_all = lnfm_done[(b, T // 128 - 1)]

        def qkv_tile(jt, seq, dest_ap, extra_w=()):
            lw = list(extra_w)
            lw.append(("pe", wq_used.get((b, seq - 2), 0)))
            wok = sd.dma("gpsimd",
                         wqb[seq % 2][:].rearrange("p (dc j) -> p dc j", dc=NDC),
                         dr["wqkvT"][:, jt * 128:(jt + 1) * 128].rearrange(
                             "(dc p) j -> p dc j", p=128),
                         waits=lw, inc="lq%d" % (seq % 2))
            if wcv_chunks:
                which, e, hf = wcv_chunks.pop(0)
                if which == 1:
                    sd.dma("gpsimd", dr["w1b_scr"][e, hf * 512:(hf + 1) * 512, :],
                           dr["w1"][e, hf * 512:(hf + 1) * 512, :], inc="wcv")
                else:
                    sd.dma("gpsimd", dr["w2b_scr"][e, hf * 2048:(hf + 1) * 2048, :],
                           dr["w2"][e, hf * 2048:(hf + 1) * 2048, :], inc="wcv")
            last = 0
            for tc in range(4):
                half = tc % 2
                def mkQ(seq, tc, half):
                    def f(eng):
                        for dc in range(NDC):
                            mi = eng.matmul(
                                ps0[:, half * 512:(half + 1) * 512],
                                wqb[seq % 2][:, dc * 128:(dc + 1) * 128],
                                lnfm3[:, dc, tc * 512:(tc + 1) * 512],
                                start=(dc == 0), stop=(dc == NDC - 1))
                        return mi
                    return f
                qw = [("lq%d" % (seq % 2), wok), ("ve", lnfm_all)] + list(extra_w)
                if tc >= 2:
                    qw.append(("sc", last))
                pe_q = sd.add("tensor", mkQ(seq, tc, half), waits=qw, inc="pe")
                wq_used[(b, seq)] = pe_q
                last = sd.add(
                    "scalar",
                    (lambda jt, tc, half, dest_ap: lambda eng: eng.add(
                        out=dest_ap[:, tc * 512:(tc + 1) * 512],
                        in_=ps0[:, half * 512:(half + 1) * 512],
                        add=bqkvs[:, jt:jt + 1]))(jt, tc, half, dest_ap),
                    waits=[("pe", pe_q)], inc="sc")
            qkv_evac[jt] = last
            return last

        kv3 = kv[:].rearrange("p (n t) -> p n t", n=16)
        for n in range(16):
            jt = 8 + n
            ew = []
            if b > 0 and n == 0:
                ew = [("pe", head_pe_end.get((b - 1, H - 1), 0)),
                      ("sc", head_exp_end.get((b - 1, H - 1), 0)),
                      ("ve", batch_prev_ve)]
            qkv_tile(jt, n, kv3[:, n, :], extra_w=ew)

        if dbg is not None:
            sd.dma("sync", dbg[b], kv[:],
                   waits=[("sc", qkv_evac[23])], inc="sG")

        for p in range(H // 2):
            ew = [("pe", head_pe_end.get((b, 2 * (p - 1) + 1), 0)),
                  ("sc", head_exp_end.get((b, 2 * (p - 1) + 1), 0)),
                  ("ve", va_done.get((b, 2 * p - 1), 0))]
            q_evac = qkv_tile(p, 16 + p, qring[p % 2][:], extra_w=ew)

            for h in (2 * p, 2 * p + 1):
                base = (h % 2) * 64
                hp2 = h % 2
                kslice = kv3[:, h // 2, :]
                vslice = kv3[:, 8 + h // 2, :]
                qslice = qring[p % 2][:]

                vw = [("sc", qkv_evac[16 + h // 2]),
                      ("ve", pv_free.get(pvi - 1, vh_set))]
                def mkV(base, vsl):
                    def f(eng):
                        for st in range(16):
                            mi = eng.transpose(
                                pv[:, st * 64:(st + 1) * 64],
                                vsl[base:base + 64, st * 128:(st + 1) * 128],
                                P["id64sbs"][base:base + 64, 0:64])
                        return mi
                    return f
                peV = sd.add("tensor", mkV(base, vslice), waits=vw, inc="pe")
                ve_vh = sd.add(
                    "vector",
                    lambda eng: eng.tensor_copy(
                        out=vh[:].rearrange("p (n c) -> p n c", n=16)[:, :, 0:64],
                        in_=pv[:].rearrange("p (n c) -> p n c", n=16)),
                    waits=[("pe", peV), ("pe", head_pe_end.get((b, h - 1), 0))],
                    inc="ve")
                pv_free[pvi] = ve_vh
                pvi += 1

                pw2 = vm_done.get((b, h - 1), (0, 0))
                sq1 = sd.add(
                    "vector",
                    (lambda base, qsl: lambda eng: eng.tensor_tensor(
                        q2b[0:64, :], qsl[base:base + 64, :],
                        qsl[base:base + 64, :], ALU.mult))(base, qslice),
                    waits=[("sc", q_evac), ("ve", pw2[0]),
                           ("pe", head_pe_end.get((b, h - 1), 0))], inc="ve")
                sq2 = sd.add(
                    "vector",
                    (lambda base, ksl: lambda eng: eng.tensor_tensor(
                        q2b[64:128, :], ksl[base:base + 64, :],
                        ksl[base:base + 64, :], ALU.mult))(base, kslice),
                    waits=[("sc", qkv_evac[8 + h // 2]), ("ve", pw2[1])], inc="ve")
                head_sc_end[(b, h)] = sd.now("sc")
                vm1 = sd.add(
                    "vector",
                    (lambda: lambda eng: eng.reduce_max(
                        rm[0:64, 0:1], q2b[0:64, :], axis=AX.X))(),
                    waits=[("ve", sq1), ("pe", head_pe_end.get((b, h - 1), 0))],
                    inc="ve")
                vm2 = sd.add(
                    "vector",
                    (lambda: lambda eng: eng.reduce_max(
                        rm[64:128, 1:2], q2b[64:128, :], axis=AX.X))(),
                    waits=[("ve", sq2)], inc="ve")
                vm_done[(b, h)] = (vm1, vm2)
                def mkC(base):
                    def f(eng):
                        eng.matmul(psav[0:1, 0:1], P["ones_cs"][0:64, :],
                                   rm[0:64, 0:1], start=True, stop=False)
                        return eng.matmul(psav[0:1, 0:1],
                                          P["ones_cs"][64:128, :],
                                          rm[64:128, 1:2],
                                          start=False, stop=True)
                    return f
                peC = sd.add("tensor", mkC(base),
                             waits=[("ve", vm2), ("ve", vm1),
                                    ("ve", va_done.get((b, h - 1), 0))], inc="pe")
                vcs = sd.add(
                    "vector",
                    lambda eng: eng.tensor_scalar(csb[:], psav[0:1, 0:1],
                                                  -0.0625, None, ALU.mult),
                    waits=[("pe", peC)], inc="ve")
                peCB = sd.add(
                    "tensor",
                    lambda eng: eng.matmul(psav[:, 8:9], ONES[0:1, 0:128],
                                           csb[:], start=True, stop=True),
                    waits=[("ve", vcs)], inc="pe")
                veEB = sd.add(
                    "vector",
                    (lambda hp2: lambda eng: eng.tensor_copy(
                        out=ebias[hp2][:], in_=psav[:, 8:9]))(hp2),
                    waits=[("pe", peCB)], inc="ve")

                for th in range(2):
                    exp_done = {}
                    av_done = {}
                    for stp in range(8):
                        sw = [("ve", veEB)]
                        if stp >= 1:
                            sw.append(("sc", exp_done[stp - 1]))
                        if stp == 0:
                            sw.append(("ve", sd.now("ve")))
                            sw.append(("sc", sd.now("sc")))
                        def mkS(base, ksl, qsl, th, stp):
                            def f(eng):
                                for sub in range(2):
                                    ps = ps0 if sub == 0 else ps1
                                    st = 2 * stp + sub
                                    for dh in range(2):
                                        mi = eng.matmul(
                                            ps[:, dh * 512:(dh + 1) * 512],
                                            ksl[base:base + 64, st * 128:(st + 1) * 128],
                                            qsl[base:base + 64,
                                                th * 1024 + dh * 512: th * 1024 + (dh + 1) * 512],
                                            start=True, stop=True)
                                return mi
                            return f
                        peS = sd.add("tensor", mkS(base, kslice, qslice, th, stp),
                                     waits=sw, inc="pe")
                        exp_done[stp] = sd.add(
                            "scalar",
                            (lambda hp2: lambda eng: eng.activation(
                                out=pT2[:], in_=ps01[:],
                                func=AF.Exp, scale=0.125,
                                bias=ebias[hp2][:]))(hp2),
                            waits=[("pe", peS),
                                   ("pe", av_done.get(stp - 1, 0))], inc="sc")
                        aw = [("sc", exp_done[stp]), ("ve", ve_vh)]
                        def mkAV(stp):
                            def f(eng):
                                for sub in range(2):
                                    st = 2 * stp + sub
                                    for dh in range(2):
                                        mi = eng.matmul(
                                            psav[0:65, dh * 512:(dh + 1) * 512],
                                            vh[:, st * 65:(st + 1) * 65],
                                            pT2[:, sub * 1024 + dh * 512:
                                                sub * 1024 + (dh + 1) * 512],
                                            start=(st == 0), stop=(st == 15))
                                return mi
                            return f
                        av_done[stp] = sd.add("tensor", mkAV(stp),
                                              waits=aw, inc="pe")
                    vr0 = sd.add(
                        "scalar",
                        lambda eng: eng.activation(
                            out=sumbc[64:65, 0:1024], in_=psav[64:65, 0:1024],
                            func=AF.Ln),
                        waits=[("pe", av_done[7])], inc="sc")
                    vr = sd.add(
                        "scalar",
                        lambda eng: eng.activation(
                            out=sumbc[64:65, 0:1024], in_=sumbc[64:65, 0:1024],
                            func=AF.Exp, scale=-1.0),
                        waits=[("sc", vr0)], inc="sc")
                    def mkBC(eng):
                        for dh in range(2):
                            mi = eng.matmul(ps0[0:64, dh * 512:(dh + 1) * 512],
                                            ONES[64:65, 0:64],
                                            sumbc[64:65, dh * 512:(dh + 1) * 512],
                                            start=True, stop=True)
                        return mi
                    peBC = sd.add("tensor", mkBC, waits=[("sc", vr)], inc="pe")
                    vbc = sd.add(
                        "vector",
                        lambda eng: eng.tensor_copy(out=sumbc[0:64, :],
                                                    in_=ps0[0:64, :]),
                        waits=[("pe", peBC)], inc="ve")
                    va = sd.add(
                        "vector",
                        (lambda base, th, hp2: lambda eng: eng.tensor_tensor(
                            avst[hp2][base:base + 64, :], psav[0:64, :],
                            sumbc[0:64, :], ALU.mult))(base, th, hp2),
                        waits=[("ve", vbc), ("sA", 1)], inc="ve")
                    avdma[(b, h, th)] = sd.dma(
                        "sync",
                        av_scr[h // 2, base:base + 64,
                               th * 1024:(th + 1) * 1024],
                        avst[hp2][base:base + 64, :],
                        waits=[("ve", va)], inc="sA")
                    va_done[(b, h)] = va
                    head_pe_end[(b, h)] = peBC
                head_exp_end[(b, h)] = sd.now("sc")

        for i in range(T // 128):
            ti = b * (T // 128) + i
            par = i % 2
            aw = [("sA", 1), ("pe", lnfm_done.get(("peP", b, i - 2), 0))]
            avok = sd.dma("sync",
                          avb[par][:].rearrange("p (dc t) -> p dc t", dc=NDC),
                          av_scr[:, :, i * 128:(i + 1) * 128].rearrange(
                              "dc p t -> p dc t"),
                          waits=aw, inc="lA%d" % par)
            if i < 2:
                xw2 = [("sc", evacA.get((b, 14 + i), 0))]
            else:
                xw2 = [("ve", lnfm_done.get(("x1", b, i - 2), 0))]
            xok = sd.dma("sync", xres[par][:],
                         dr["x"][xoff + i * 128: xoff + (i + 1) * 128, :],
                         waits=xw2, inc="lx%d" % par)
            woT3 = woT[:].rearrange("p (dc c) -> p dc c", dc=NDC)
            def mkP(par):
                def f(eng):
                    for dh in range(2):
                        for dc in range(NDC):
                            eng.matmul(ps1[:, dh * 512:(dh + 1) * 512],
                                       avb[par][:, dc * 128:(dc + 1) * 128],
                                       woT3[:, dc, dh * 512:(dh + 1) * 512],
                                       start=(dc == 0), stop=False)
                        mi = eng.matmul(ps1[:, dh * 512:(dh + 1) * 512],
                                        P["oh8bs"][:, 0:128],
                                        rows8b[:, dh * 512:(dh + 1) * 512],
                                        start=False, stop=True)
                    return mi
                return f
            pw = [("lA%d" % par, avok), ("wc", wl_consts)]
            if i == 0:
                pw.append(("pe", head_pe_end[(b, H - 1)]))
                pw.append(("sc", head_sc_end[(b, H - 1)]))
            if i >= 1:
                pw.append(("ve", lnfm_done.get(("prjc", b, i - 1), 0)))
            peP = sd.add("tensor", mkP(par), waits=pw, inc="pe")
            proj_end_pe = peP
            lnfm_done[("peP", b, i)] = peP
            vpc = sd.add("vector",
                         (lambda par: lambda eng: eng.tensor_copy(
                             out=prj[par][:], in_=ps1[:]))(par),
                         waits=[("pe", peP),
                                ("sc", lnfm_done.get(("sc_d", b, i - 2), 0))],
                         inc="ve")
            lnfm_done[("prjc", b, i)] = vpc
            v1 = sd.add("vector",
                        (lambda par: lambda eng: eng.bn_stats(
                            bnb[par][:, 0:6], prj[par][:, 0:512]))(par),
                        waits=[("ve", vpc)], inc="ve")
            v2 = sd.add("vector",
                        (lambda par: lambda eng: eng.bn_stats(
                            bnb[par][:, 6:12], prj[par][:, 512:1024]))(par),
                        inc="ve")
            v3 = sd.add("vector",
                        (lambda par: lambda eng: eng.bn_aggr(
                            st2[par][:], bnb[par][:, 0:12]))(par),
                        waits=[("ve", v2)], inc="ve")
            s1 = sd.add("scalar",
                        (lambda par: lambda eng: eng.activation(
                            out=rstd[par][:], in_=st2[par][:, 1:2], func=AF.Ln,
                            bias=P["epsas"][:]))(par),
                        waits=[("ve", v3)], inc="sc")
            s2 = sd.add("scalar",
                        (lambda par: lambda eng: eng.activation(
                            out=rstd[par][:], in_=rstd[par][:], func=AF.Exp,
                            scale=-0.5))(par),
                        waits=[("sc", s1)], inc="sc")
            v4 = sd.add("vector",
                        (lambda par: lambda eng: eng.scalar_tensor_tensor(
                            bnc[par][:], st2[par][:, 0:1], -1.0, rstd[par][:],
                            ALU.mult, ALU.mult))(par),
                        waits=[("sc", s2)], inc="ve")
            v5 = sd.add("vector",
                        (lambda par: lambda eng: eng.tensor_scalar(
                            xres[par][:], xres[par][:], bnc[par][:], None,
                            ALU.add))(par),
                        waits=[("ve", v4), ("lx%d" % par, xok)], inc="ve")
            v6 = sd.add("vector",
                        (lambda par: lambda eng: eng.scalar_tensor_tensor(
                            x1s[par][:], prj[par][:], rstd[par][:], xres[par][:],
                            ALU.mult, ALU.add))(par),
                        waits=[("ve", v5), ("sX1", 1)], inc="ve")
            lnfm_done[("x1", b, i)] = v6
            sd.dma("sync", x1_scr[ti * 128:(ti + 1) * 128, :], x1s[par][:],
                   waits=[("ve", v6)], inc="sX1")
            w1_ = [("ve", v6)]
            u1 = sd.add("vector",
                        (lambda par: lambda eng: eng.bn_stats(
                            bnb[par][:, 0:6], x1s[par][:, 0:512]))(par),
                        waits=w1_, inc="ve")
            u2 = sd.add("vector",
                        (lambda par: lambda eng: eng.bn_stats(
                            bnb[par][:, 6:12], x1s[par][:, 512:1024]))(par),
                        inc="ve")
            u3 = sd.add("vector",
                        (lambda par: lambda eng: eng.bn_aggr(
                            st2[par][:], bnb[par][:, 0:12]))(par),
                        waits=[("ve", u2)], inc="ve")
            t1 = sd.add("scalar",
                        (lambda par: lambda eng: eng.activation(
                            out=rstd[par][:], in_=st2[par][:, 1:2], func=AF.Ln,
                            bias=P["epsas"][:]))(par),
                        waits=[("ve", u3)], inc="sc")
            t2 = sd.add("scalar",
                        (lambda par: lambda eng: eng.activation(
                            out=rstd[par][:], in_=rstd[par][:], func=AF.Exp,
                            scale=-0.5))(par),
                        waits=[("sc", t1)], inc="sc")
            u4 = sd.add("vector",
                        (lambda par: lambda eng: eng.scalar_tensor_tensor(
                            bnc[par][:], st2[par][:, 0:1], -1.0, rstd[par][:],
                            ALU.mult, ALU.mult))(par),
                        waits=[("sc", t2)], inc="ve")
            t3 = sd.add("scalar",
                        (lambda par: lambda eng: eng.activation(
                            out=x2s[par][:], in_=x1s[par][:], func=AF.Identity,
                            scale=rstd[par][:], bias=bnc[par][:]))(par),
                        waits=[("ve", u4), ("sX2", 1)], inc="sc")
            lnfm_done[("sc_d", b, i)] = t3
            sd.dma("sync", x2_scr[ti * 128:(ti + 1) * 128, :], x2s[par][:],
                   waits=[("sc", t3)], inc="sX2")
            tw = [("sc", t3), ("ve", lnfm_done.get(("x2c", b, i - 1), 0))]
            def mkTX(par):
                def f(eng):
                    for dc in range(NDC):
                        mi = eng.transpose(psav[:, dc * 128:(dc + 1) * 128],
                                           x2s[par][:, dc * 128:(dc + 1) * 128],
                                           P["id128fs"][:])
                    return mi
                return f
            peTX = sd.add("tensor", mkTX(par), waits=tw, inc="pe")
            vx = sd.add("vector",
                        (lambda par: lambda eng: eng.tensor_copy(
                            out=x2fT[par][:], in_=psav[:]))(par),
                        waits=[("pe", peTX),
                               ("pe", lnfm_done.get(("peR", b, i - 1), 0))],
                        inc="ve")
            lnfm_done[("x2c", b, i)] = vx
            def mkR(par):
                def f(eng):
                    for dc in range(NDC):
                        mi = eng.matmul(ps0[:, 0:8],
                                        x2fT[par][:, dc * 128:(dc + 1) * 128],
                                        rws[:, dc * E:(dc + 1) * E],
                                        start=(dc == 0), stop=(dc == NDC - 1))
                    return mi
                return f
            peR = sd.add("tensor", mkR(par), waits=[("ve", vx)], inc="pe")
            lnfm_done[("peR", b, i)] = peR
            gpar = (ti // TPG) % 2
            vc = sd.add(
                "vector",
                (lambda par, gpar, i: lambda eng: eng.tensor_copy(
                    out=x2fg[gpar][:].rearrange("p (dc t) -> p dc t", dc=NDC)[
                        :, :, (i % TPG) * 128:(i % TPG + 1) * 128],
                    in_=x2fT[par][:].rearrange("p (dc t) -> p dc t", dc=NDC)))(
                        par, gpar, i),
                waits=[("ve", vx), ("sF", 1)],
                inc="ve")
            if ti % TPG == TPG - 1:
                g = ti // TPG
                x2f_stored[g] = sd.dma(
                    "sync", x2fm_scr[g].rearrange("dc p t -> p dc t"),
                    x2fg[gpar][:].rearrange("p (dc t) -> p dc t", dc=NDC),
                    waits=[("ve", vc)], inc="sF")
            gv1 = sd.add("vector",
                         lambda eng: eng.tensor_copy(out=glog[:], in_=ps0[:, 0:8]),
                         waits=[("pe", peR)], inc="ve")
            gv2 = sd.add("vector",
                         lambda eng: eng.reduce_max(gm1[:], glog[:], axis=AX.X),
                         waits=[("ve", gv1)], inc="ve")
            gv3 = sd.add("vector",
                         lambda eng: eng.tensor_scalar(gmask1[:], glog[:], gm1[:],
                                                       None, ALU.is_equal),
                         waits=[("ve", gv2)], inc="ve")
            gv4 = sd.add("vector",
                         lambda eng: eng.scalar_tensor_tensor(
                             gmask2[:], gmask1[:], -BIG, glog[:], ALU.mult,
                             ALU.add),
                         waits=[("ve", gv3)], inc="ve")
            gv5 = sd.add("vector",
                         lambda eng: eng.reduce_max(gm2[:], gmask2[:], axis=AX.X),
                         waits=[("ve", gv4)], inc="ve")
            gv6 = sd.add("vector",
                         lambda eng: eng.tensor_tensor(gden[:], gm2[:], gm1[:],
                                                       ALU.subtract),
                         waits=[("ve", gv5)], inc="ve")
            gs1 = sd.add("scalar",
                         lambda eng: eng.activation(out=ge2[:], in_=gden[:],
                                                    func=AF.Exp),
                         waits=[("ve", gv6)], inc="sc")
            gv7 = sd.add("vector",
                         lambda eng: eng.tensor_scalar(gden[:], ge2[:], 1.0,
                                                       None, ALU.add),
                         waits=[("sc", gs1)], inc="ve")
            gv8 = sd.add("vector",
                         lambda eng: eng.reciprocal(out=gden[:], in_=gden[:]),
                         waits=[("ve", gv7)], inc="ve")
            gv9 = sd.add("vector",
                         lambda eng: eng.tensor_tensor(ge2[:], ge2[:], gden[:],
                                                       ALU.mult),
                         waits=[("ve", gv8)], inc="ve")
            gv10 = sd.add("vector",
                          lambda eng: eng.tensor_scalar(g1c[:], ge2[:], -1.0,
                                                        1.0, ALU.mult, ALU.add),
                          waits=[("ve", gv9)], inc="ve")
            gv11 = sd.add("vector",
                          lambda eng: eng.tensor_scalar(gmask2[:], glog[:],
                                                        gm2[:], None,
                                                        ALU.is_equal),
                          waits=[("ve", gv9)], inc="ve")
            gv12 = sd.add("vector",
                          lambda eng: eng.tensor_scalar(gmask1[:], gmask1[:],
                                                        g1c[:], None, ALU.mult),
                          waits=[("ve", gv10)], inc="ve")
            gv13 = sd.add("vector",
                          lambda eng: eng.tensor_scalar(gmask2[:], gmask2[:],
                                                        ge2[:], None, ALU.mult),
                          waits=[("ve", gv11)], inc="ve")
            gv14 = sd.add("vector",
                          (lambda ti: lambda eng: eng.tensor_tensor(
                              P["gates_sb"][:, ti * E:(ti + 1) * E], gmask1[:],
                              gmask2[:], ALU.add))(ti),
                          waits=[("ve", gv13), ("ve", gv12)], inc="ve")
            peG = sd.add("tensor",
                         (lambda ti: lambda eng: eng.transpose(
                             ps0[0:8, 128:256],
                             P["gates_sb"][:, ti * E:(ti + 1) * E],
                             P["id128fs"][:]))(ti),
                         waits=[("ve", gv14)], inc="pe")
            gv15 = sd.add("vector",
                          (lambda par: lambda eng: eng.tensor_copy(
                              out=gfmst[par][:], in_=ps0[0:8, 128:256]))(par),
                          waits=[("pe", peG), ("sG", 1)],
                          inc="ve")
            gfm_used2[ti] = sd.dma(
                "sync", gatesfm_scr[:, ti * 128:(ti + 1) * 128], gfmst[par][:],
                waits=[("ve", gv15)], inc="sG")
        batch_prev_ve = sd.now("ve")

    ast.close()
    return sd.barrier_counts()


def _prep_inputs(inputs):
    x = np.ascontiguousarray(np.asarray(inputs["L0_attn_x"], np.float32))
    Wqkv = np.asarray(inputs["L0_attn_W_qkv"], np.float32)
    bqkv = np.asarray(inputs["L0_attn_b_qkv"], np.float32)
    Wout = np.asarray(inputs["L0_attn_W_out"], np.float32)
    bout = np.asarray(inputs["L0_attn_b_out"], np.float32)

    perm = []
    for comp in range(3):
        for h in range(H):
            perm.extend(range(h * 192 + comp * 64, h * 192 + comp * 64 + 64))
    perm = np.array(perm)
    WqkvT = np.ascontiguousarray(Wqkv[perm].T)
    bqkv_cols = np.ascontiguousarray(bqkv[perm].reshape(NJT, 128).T)
    rows8 = np.zeros((E, C), np.float32)
    rows8[0] = bout
    b1 = np.asarray(inputs["L0_ffn_b1"], np.float32)
    oh8 = np.zeros((E, E * 128), np.float32)
    for e in range(E):
        oh8[e, e * 128:(e + 1) * 128] = 1.0

    common = {
        "wqkvT": WqkvT,
        "bqkv": bqkv_cols,
        "woutT": np.ascontiguousarray(Wout.T),
        "rows8": rows8,
        "rw": np.ascontiguousarray(np.asarray(inputs["L0_ffn_router_w"], np.float32)),
        "w1": np.ascontiguousarray(np.asarray(inputs["L0_ffn_w1"], np.float32)),
        "b1c": np.ascontiguousarray(
            b1.reshape(E, NFC, 128).transpose(2, 0, 1).reshape(128, E * NFC)),
        "w2": np.ascontiguousarray(np.asarray(inputs["L0_ffn_w2"], np.float32)),
        "b2": np.ascontiguousarray(np.asarray(inputs["L0_ffn_b2"], np.float32)),
        "gam": np.ascontiguousarray(np.asarray(inputs["L0_ffn_gamma"], np.float32)),
        "bet": np.ascontiguousarray(np.asarray(inputs["L0_ffn_beta"], np.float32)),
        "allones": np.ones((128, 128), np.float32),
        "ones_col": np.ones((128, 1), np.float32),
        "oh8f": oh8,
        "oh8b": oh8.astype(ml_dtypes.bfloat16),
        "id128b": np.eye(128, dtype=ml_dtypes.bfloat16),
        "id128f": np.eye(128, dtype=np.float32),
        "id64sb": np.vstack([np.eye(64), np.eye(64)]).astype(ml_dtypes.bfloat16),
        "epsa": np.full((128, 1), EPS_A, np.float32),
        "epsm": np.full((128, 1), EPS_M, np.float32),
    }
    shards = []
    for c in range(NCORES):
        m = dict(common)
        m["x"] = np.ascontiguousarray(x[c * BPC:(c + 1) * BPC].reshape(NTOK, C))
        shards.append(m)
    return shards


def kernel(**inputs):
    shards = _prep_inputs(inputs)
    nc = build_nc(dev_moe_only=False)
    res = run_bass_kernel_spmd(nc, shards, core_ids=list(range(NCORES)))
    outs = [r["out"].reshape(BPC, T, C) for r in res.results]
    return np.concatenate(outs, axis=0)
